# version 04
# baseline (speedup 1.0000x reference)
import contextlib
import numpy as np

import concourse.bass as bass
import concourse.mybir as mybir
from concourse.bass_utils import run_bass_kernel_spmd

F32 = mybir.dt.float32
AF = mybir.ActivationFunctionType
ALU = mybir.AluOpType
AX = mybir.AxisListType

B, L, HID = 2, 8192, 4096
H, P, N, G, KCONV, CS = 128, 64, 128, 8, 4, 256
INTER = H * P
CONVC = INTER + 2 * G * N
PROJ = INTER + CONVC + H
EPS = 1e-5
NCORES = 8
TOK = (B * L) // NCORES
NCH = TOK // CS
HALO = 3
DC = HID // 128
NCC = PROJ // 128
OC = HID // 128
KK = INTER // 128
W0 = CS + HALO
NEG = -1e30
GRP_RG = [[0, 1, 2, 3], [4, 5, 6, 7]]
PHASE = 2
SUB = 9
SSDSUB = 9
T_, V_, S_, G_, Y_ = "tensor", "vector", "scalar", "gpsimd", "sync"


class Emitter:

    def __init__(self):
        self.ops = {k: [] for k in (T_, V_, S_, G_, Y_)}
        self.ctr = {}
        self.hwm = {}
        self.h = {}

    def add_sem(self, name, handle):
        self.h[name] = handle
        self.ctr[name] = 0

    def wait(self, eng, semname, val=None):
        if val is None:
            val = self.ctr[semname]
        if val is None or val <= 0:
            return
        key = (eng, semname)
        if self.hwm.get(key, 0) >= val:
            return
        self.hwm[key] = val
        hh = self.h[semname]
        self.ops[eng].append(lambda e, hh=hh, v=val: e.wait_ge(hh, v))

    CHAIN = {"vector": "eV", "scalar": "eS", "gpsimd": "eG"}

    def do(self, eng, fn, inc=None):
        cs = self.CHAIN.get(eng)
        if cs is not None and not (isinstance(inc, tuple)):
            self.wait(eng, cs, self.ctr[cs])
            if inc is None:
                inc = cs
        if inc is None:
            self.ops[eng].append(fn)
            return None
        name, step = (inc, 1) if isinstance(inc, str) else inc
        self.ctr[name] += step
        hh = self.h[name]

        def wrapped(e, fn=fn, hh=hh, step=step):
            fn(e).then_inc(hh, step)
        self.ops[eng].append(wrapped)
        return self.ctr[name]


def build(nc, debug=False):
    es = contextlib.ExitStack()

    dram_in = lambda n, s: nc.dram_tensor(n, s, F32, kind="ExternalInput").ap()
    x_fm = dram_in("x_fm", [HID, TOK + HALO])
    Wp = dram_in("Wp", [NCC, 2, 128, 16, 128])
    Wo = dram_in("Wo", [OC, 4, 128, 16, 128])
    A_ls = dram_in("A_ls", [2, 128, 16, 128])
    B_as = dram_in("B_as", [2, 128, 16, 128])
    WrT = dram_in("WrT", [HID, 8])
    brv = dram_in("brv", [1, 8])
    Emat = dram_in("Emat", [8, 128])
    convw = dram_in("convw", [128, 80 * KCONV])
    convb = dram_in("convb", [128, 80])
    dtbias = dram_in("dtbias", [128, 1])
    Alog = dram_in("Alog", [128, 1])
    Dvin = dram_in("Dvin", [1, 128])
    normw = dram_in("normw", [128, KK])
    selmask = dram_in("selmask", [1, 16])
    jmaskv = dram_in("jmaskv", [1, 4])

    odram = nc.dram_tensor("out", [NCH, OC, 128, CS], F32, kind="ExternalOutput").ap()
    kd = dict(kind="ExternalOutput") if debug else {}
    ydram = nc.dram_tensor("ydram", [NCH, KK, 128, CS], F32, **kd).ap()
    gdram = nc.dram_tensor("gdram", [NCH, KK, 128, CS], F32, **kd).ap()
    xmdram = nc.dram_tensor("xmdram", [NCH, DC, 128, CS], F32, **kd).ap()
    cdram = nc.dram_tensor("cdram", [NCH, G, 128, CS], F32, **kd).ap()
    gcdram = nc.dram_tensor("gcdram", [NCH, 128, CS], F32, **kd).ap()
    ccin = nc.dram_tensor("ccin", [128, INTER + 64], F32, **kd).ap()
    ccout = nc.dram_tensor("ccout", [4, 128, INTER + 64], F32, **kd).ap()

    sbt = lambda n, cols, parts=128: es.enter_context(
        nc.sbuf_tensor(n, [parts, cols], F32))

    ones1 = sbt("ones1", 128, 1)
    onescol = sbt("onescol", 1)
    ident = sbt("ident", 128)
    mask = sbt("mask", CS)
    E_sb = sbt("E_sb", 128, 8)
    WrT_sb = sbt("WrT_sb", DC * 8)
    br_b = sbt("br_b", 8)
    convw_sb = sbt("convw_sb", 80 * KCONV)
    convb_sb = sbt("convb_sb", 80)
    dtb_sb = sbt("dtb_sb", 1)
    Alog_sb = sbt("Alog_sb", 1)
    negA = sbt("negA", 1)
    Drow = sbt("Drow", 128, 1)
    D_b = sbt("D_b", 128)
    normw_sb = sbt("normw_sb", KK)
    brrow = sbt("brrow", 8, 1)
    smrow = sbt("smrow", 16, 1)
    jmrow = sbt("jmrow", 4, 1)
    R_sb = sbt("R_sb", 1)
    epsb = sbt("epsb", 1)
    slocal = sbt("slocal", INTER)
    Lcols = sbt("Lcols", 4)
    coefw = sbt("coefw", 2)
    selm_b = sbt("selm_b", 16)
    jm_b = sbt("jm_b", 4)
    coefcols = sbt("coefcols", 4)
    coefb = sbt("coefb", 512)
    coefrow = None

    P1 = sbt("P1", 2 * DC * W0)
    ZOFF, XOFF = 0, DC * W0
    hb = [sbt(f"hb{i}", 16 * 128) for i in range(3)]
    AR2 = sbt("AR2", DC * CS)
    xfb = AR2[:, 0:DC * CS]
    rawsb = AR2[:, 0:G * 384]
    Btm = AR2[:, 3072:3072 + G * CS]
    _o = 5120
    dtw = AR2[:, _o:_o + CS]; _o += CS
    dtw2 = AR2[:, _o:_o + CS]; _o += CS
    dt_fm = AR2[:, _o:_o + CS]; _o += CS
    cum_fm = AR2[:, _o:_o + CS]; _o += CS
    dlast = AR2[:, _o:_o + CS]; _o += CS
    ddl_fm = AR2[:, _o:_o + CS]; _o += CS
    gce_fm = AR2[:, _o:_o + CS]; _o += CS
    dt_tm = AR2[:, _o:_o + CS]; _o += CS
    ddl_tm = AR2[:, _o:_o + CS]; _o += CS
    ncm_tm = AR2[:, _o:_o + CS]; _o += CS
    hsband = [AR2[:, _o:_o + CS], AR2[:, _o + CS:_o + 2 * CS]]
    assert _o + 2 * CS == DC * CS
    sjq = [AR2[:, 0:2048], AR2[:, 2048:4096]]
    cdec = sbt("cdec", 1)
    cdecrow = sbt("cdecrow", 128, 1)
    cdec_b = sbt("cdec_b", 128)
    cumrow = sbt("cumrow", 8 * CS, 1)
    gcrow = cumrow
    coefrow = cumrow[0:1, 0:512]
    xch = [sbt(f"xch{i}", 128) for i in range(2)]
    ecb = sbt("ecb", CS)
    d1w = sbt("d1w", 128)
    L0w = sbt("L0w", CS)
    L1w = sbt("L1w", 128)
    sT0 = sbt("sT0", CS)
    sT1 = sbt("sT1", 128)
    bp0 = d1w[:, 0:128]
    bp1 = L1w[:, 0:128]
    xbcw = [sbt(f"xbcw{i}", W0) for i in range(2)]
    xbcw2 = [dtw[:, 0:CS], dtw2[:, 0:CS]]
    cvw = [sbt(f"cvw{i}", W0) for i in range(2)]
    tails = sbt("tails", 80 * HALO)
    sqw = [c[:, 0:W0] for c in cvw]
    BCfm = sbt("BCfm", 2 * G * CS)
    BOFF, COFF = 0, G * CS
    lowsb = sbt("lowsb", W0)
    d0w = lowsb[:, 0:CS]
    srow_tm = sbt("srow_tm", 8)
    srow8 = sbt("srow8", W0, 8)
    statc = sbt("statc", 4)
    stat = [sbt(f"stat{i}", W0, 1) for i in range(4)]
    ysta = [sbt(f"ysta{i}", CS) for i in range(2)]
    gsta = [y[:, 0:CS] for y in ysta]
    xm_sb = [y[:, 0:CS] for y in ysta]
    gbsb = gsta[0][:, 0:CS]
    csb2 = gsta[1][:, 0:CS]
    t1w = sbt("t1w", W0)
    sexp_sb = t1w[:, 0:W0]
    ybuf = BCfm[:, 0:8 * CS]
    gbuf = BCfm[:, 8 * CS:16 * CS]

    psr = [es.enter_context(nc.psum_tensor(f"pk{i}", [128, 512], F32))
           for i in range(8)]

    em = Emitter()
    for name in ("dS", "dX", "dH0", "dH1", "dH2", "dYS0", "dYS1", "dGS0",
                 "dGS1", "dXM", "dCS", "dGC", "dFL", "dL2", "dXL", "dSJ",
                 "dCC", "dO", "eT", "eV", "eS", "eG", "cc"):
        em.add_sem(name, es.enter_context(nc.semaphore(name)))

    zb = P1[:, ZOFF:ZOFF + DC * W0].rearrange("p (c t) -> p c t", c=DC)
    xb = P1[:, XOFF:XOFF + DC * W0].rearrange("p (c t) -> p c t", c=DC)
    gfull = P1[:, 0:KK * CS].rearrange("p (c t) -> p c t", c=KK)
    z3 = P1[:, XOFF:XOFF + DC * CS].rearrange("p (d t) -> p d t", d=DC)

    em.do(V_, lambda e: e.memset(ones1[:], 1.0), inc="eV")
    em.do(V_, lambda e: e.memset(onescol[:], 1.0), inc="eV")
    em.do(V_, lambda e: e.memset(R_sb[:], 0.0), inc="eV")
    em.do(V_, lambda e: e.memset(slocal[:], 0.0), inc="eV")
    em.do(V_, lambda e: e.memset(epsb[:], EPS), inc="eV")
    em.do(G_, lambda e: e.memset(ident[:], 0.0), inc="eG")
    em.wait(G_, "eG")
    em.do(G_, lambda e: e.affine_select(
        out=ident[:], in_=ident[:], compare_op=ALU.not_equal, fill=1.0,
        base=0, pattern=[[-1, 128]], channel_multiplier=1), inc="eG")
    em.do(G_, lambda e: e.memset(mask[:], 0.0), inc="eG")
    em.wait(G_, "eG")
    em.do(G_, lambda e: e.affine_select(
        out=mask[:], in_=mask[:], compare_op=ALU.is_ge, fill=NEG,
        base=0, pattern=[[1, CS]], channel_multiplier=-1), inc="eG")
    for dst, src in [
        (E_sb[:], Emat), (convw_sb[:], convw), (convb_sb[:], convb),
        (dtb_sb[:], dtbias), (normw_sb[:], normw), (Alog_sb[:], Alog),
        (Drow[:], Dvin), (brrow[:], brv), (smrow[:], selmask),
        (jmrow[:], jmaskv),
    ]:
        em.do(Y_, lambda e, d=dst, s=src: e.dma_start(out=d, in_=s),
              inc=("dS", 16))
    em.do(Y_, lambda e: e.dma_start(
        out=WrT_sb[:].rearrange("p (c j) -> p c j", j=8),
        in_=WrT.rearrange("(c p) j -> p c j", p=128)), inc=("dS", 16))
    em.wait(S_, "dS")
    em.do(S_, lambda e: e.activation(negA[:], Alog_sb[:], AF.Exp), inc="eS")
    em.wait(V_, "eS")
    em.do(V_, lambda e: e.tensor_scalar(negA[:], negA[:], -1.0, None, ALU.mult),
          inc="eV")
    em.wait(T_, "dS")
    em.wait(T_, "eV")
    for dst, src, w in [(D_b, Drow, 128), (br_b, brrow, 8),
                        (selm_b, smrow, 16), (jm_b, jmrow, 4)]:
        em.do(T_, lambda e, s=src, w=w: e.matmul(
            psr[4][:, 0:w], ones1[0:1, :], s[0:1, 0:w], start=True, stop=True),
            inc="eT")
        em.wait(V_, "eT")
        em.do(V_, lambda e, d=dst, w=w: e.tensor_copy(d[:, 0:w], psr[4][:, 0:w]),
              inc="eV")
        em.wait(T_, "eV")
    em.wait(T_, "eG")
    em.wait(V_, "eG")
    em.wait(S_, "eG")

    def ln_layer(src_of_dc, Wt, dst_of_dc):
        em.wait(T_, "eV")
        em.wait(T_, "eS")
        em.wait(S_, "eT")
        em.wait(S_, "eV")
        sq_marks = {}
        for dc in range(DC):
            if dc >= 2:
                em.wait(S_, "eT", sq_marks[dc - 2])
            em.do(S_, lambda e, dc=dc: e.activation(
                sqw[dc % 2][:, 0:Wt], src_of_dc(dc), AF.Square), inc="eS")
            em.do(T_, lambda e, dc=dc: e.matmul(
                psr[4][0:1, 0:Wt], onescol[:], src_of_dc(dc),
                start=(dc == 0), stop=(dc == DC - 1)), inc="eT")
            em.wait(T_, "eS")
            sq_marks[dc] = em.do(T_, lambda e, dc=dc: e.matmul(
                psr[5][0:1, 0:Wt], onescol[:], sqw[dc % 2][:, 0:Wt],
                start=(dc == 0), stop=(dc == DC - 1)), inc="eT")
        em.wait(V_, "eT")
        em.do(V_, lambda e: e.tensor_scalar(
            stat[0][0:1, 0:Wt], psr[4][0:1, 0:Wt], 1.0 / HID, None, ALU.mult),
            inc="eV")
        em.do(V_, lambda e: e.tensor_mul(
            stat[1][0:1, 0:Wt], stat[0][0:1, 0:Wt], stat[0][0:1, 0:Wt]),
            inc="eV")
        em.do(V_, lambda e: e.scalar_tensor_tensor(
            stat[2][0:1, 0:Wt], psr[5][0:1, 0:Wt], 1.0 / HID, stat[1][0:1, 0:Wt],
            ALU.mult, ALU.subtract), inc="eV")
        em.wait(S_, "eV")
        em.do(S_, lambda e: e.activation(
            stat[3][0:1, 0:Wt], stat[2][0:1, 0:Wt], AF.Ln,
            bias=epsb[0:1, :]), inc="eS")
        em.do(S_, lambda e: e.activation(
            stat[3][0:1, 0:Wt], stat[3][0:1, 0:Wt], AF.Exp, scale=-0.5),
            inc="eS")
        em.wait(V_, "eS")
        em.do(V_, lambda e: e.scalar_tensor_tensor(
            stat[1][0:1, 0:Wt], stat[0][0:1, 0:Wt], -1.0, stat[3][0:1, 0:Wt],
            ALU.mult, ALU.mult), inc="eV")
        em.wait(T_, "eV")
        em.wait(T_, "eS")
        em.do(T_, lambda e: e.matmul(
            psr[4][:, 0:Wt], ones1[0:1, :], stat[3][0:1, 0:Wt],
            start=True, stop=True), inc="eT")
        em.do(T_, lambda e: e.matmul(
            psr[5][:, 0:Wt], ones1[0:1, :], stat[1][0:1, 0:Wt],
            start=True, stop=True), inc="eT")
        em.wait(V_, "eT")
        for dc in range(DC):
            em.do(V_, lambda e, dc=dc: e.tensor_tensor(
                dst_of_dc(dc), src_of_dc(dc), psr[4][:, 0:Wt], ALU.mult),
                inc="eV")
            em.do(V_, lambda e, dc=dc: e.tensor_tensor(
                dst_of_dc(dc), dst_of_dc(dc), psr[5][:, 0:Wt], ALU.add),
                inc="eV")

    def ffn(Wt, zsrc, res_view):
        em.wait(T_, "eV")
        em.wait(T_, "eS")
        tbs = [(i * 128, min(128, Wt - i * 128)) for i in range((Wt + 127) // 128)]
        for tb, (t0, tw) in enumerate(tbs):
            for dc in range(DC):
                em.do(T_, lambda e, tb=tb, t0=t0, tw=tw, dc=dc: e.matmul(
                    psr[6][0:tw, 8 * tb:8 * tb + 8],
                    zsrc(dc)[:, t0:t0 + tw], WrT_sb[:, 8 * dc:8 * dc + 8],
                    start=(dc == 0), stop=(dc == DC - 1)), inc="eT")
        for tb, (t0, tw) in enumerate(tbs):
            em.wait(V_, "eT")
            em.do(V_, lambda e, tb=tb, tw=tw: e.tensor_tensor(
                srow_tm[0:tw, 0:8], psr[6][0:tw, 8 * tb:8 * tb + 8],
                br_b[0:tw, :], ALU.add), inc="eV")
            em.do(V_, lambda e, tw=tw: e.reduce_max(
                statc[0:tw, 0:1], srow_tm[0:tw, 0:8], axis=AX.X, negate=True),
                inc="eV")
            em.wait(S_, "eV")
            em.do(S_, lambda e, tw=tw: e.activation(
                srow_tm[0:tw, 0:8], srow_tm[0:tw, 0:8], AF.Exp,
                bias=statc[0:tw, 0:1]), inc="eS")
            em.wait(V_, "eS")
            em.do(V_, lambda e, tw=tw: e.reduce_sum(
                statc[0:tw, 1:2], srow_tm[0:tw, 0:8], axis=AX.X), inc="eV")
            em.do(V_, lambda e, tw=tw: e.reciprocal(
                statc[0:tw, 1:2], statc[0:tw, 1:2]), inc="eV")
            em.do(V_, lambda e, tw=tw: e.tensor_scalar(
                srow_tm[0:tw, 0:8], srow_tm[0:tw, 0:8], statc[0:tw, 1:2], 8.0,
                ALU.mult, ALU.mult), inc="eV")
            em.wait(T_, "eV")
            em.do(T_, lambda e, t0=t0, tw=tw: e.transpose(
                psr[7][0:8, t0:t0 + tw], srow_tm[0:tw, 0:8], ident[0:tw, 0:tw]),
                inc="eT")
        em.wait(V_, "eT")
        em.do(V_, lambda e: e.tensor_copy(srow8[0:8, 0:Wt], psr[7][0:8, 0:Wt]),
              inc="eV")
        em.wait(Y_, "eT")
        for q in range(2):
            em.do(Y_, lambda e, q=q: e.dma_start(
                out=hb[q][:], in_=A_ls[q].rearrange("p s c -> p (s c)")),
                inc=(f"dH{q}", 16))
        em.wait(T_, "dH0")
        em.wait(T_, "dH1")
        em.wait(T_, "eV")
        for q in range(2):
            for s in range(16):
                dc = 16 * q + s
                em.do(T_, lambda e, q=q, s=s, dc=dc: e.matmul(
                    psr[0][:, 0:Wt], hb[q][:, 128 * s:128 * s + 128],
                    zsrc(dc), start=(dc == 0), stop=(dc == DC - 1)), inc="eT")
        em.do(T_, lambda e: e.matmul(
            psr[1][:, 0:Wt], E_sb[0:8, :], srow8[0:8, 0:Wt],
            start=True, stop=True), inc="eT")
        em.wait(V_, "eT")
        em.do(V_, lambda e: e.tensor_copy(sexp_sb[:, 0:Wt], psr[1][:, 0:Wt]),
              inc="eV")
        em.do(V_, lambda e: e.tensor_tensor(
            lowsb[:, 0:Wt], psr[0][:, 0:Wt], sexp_sb[:, 0:Wt], ALU.mult),
            inc="eV")
        em.wait(Y_, "eT")
        for q in range(2):
            em.do(Y_, lambda e, q=q: e.dma_start(
                out=hb[q][:], in_=B_as[q].rearrange("p s c -> p (s c)")),
                inc=(f"dH{q}", 16))
        em.wait(T_, "dH0")
        em.wait(T_, "dH1")
        em.wait(T_, "eV")
        ev_marks = {}
        for oc in range(OC):
            q, s = oc // 16, oc % 16
            bank = 2 + oc % 2
            if oc >= 2:
                em.wait(T_, "eV", ev_marks[oc - 2])
            upm = em.do(T_, lambda e, q=q, s=s, bank=bank: e.matmul(
                psr[bank][:, 0:Wt], hb[q][:, 128 * s:128 * s + 128],
                lowsb[:, 0:Wt], start=True, stop=True), inc="eT")
            em.wait(V_, "eT", upm)
            ev_marks[oc] = em.do(V_, lambda e, oc=oc, bank=bank: e.tensor_tensor(
                res_view(oc), res_view(oc), psr[bank][:, 0:Wt], ALU.add),
                inc="eV")

    def conv_and_dest(cc_conv, bank, mmk, Wt, off, tile, dest_fn):
        wtile = xbcw[cc_conv % 2]
        cw = cvw[cc_conv % 2]
        em.wait(V_, "eS")
        if tile > 0:
            em.do(V_, lambda e: e.tensor_copy(
                wtile[:, 0:HALO],
                tails[:, HALO * cc_conv:HALO * cc_conv + HALO]), inc="eV")
        em.wait(S_, "eT", mmk)
        em.wait(S_, "eV")
        emk = em.do(S_, lambda e: e.activation(
            wtile[:, HALO - off:HALO - off + Wt], psr[bank][:, 0:Wt], AF.Copy),
            inc="eS")
        em.wait(V_, "eS", emk)
        em.do(V_, lambda e: e.tensor_copy(
            tails[:, HALO * cc_conv:HALO * cc_conv + HALO],
            wtile[:, CS:CS + HALO]), inc="eV")
        em.do(V_, lambda e: e.tensor_scalar(
            cw[:, 0:CS], wtile[:, 0:CS],
            convw_sb[:, 4 * cc_conv:4 * cc_conv + 1], None, ALU.mult), inc="eV")
        for kk_ in range(1, KCONV):
            em.do(V_, lambda e, kk_=kk_: e.scalar_tensor_tensor(
                cw[:, 0:CS], wtile[:, kk_:kk_ + CS],
                convw_sb[:, 4 * cc_conv + kk_:4 * cc_conv + kk_ + 1],
                cw[:, 0:CS], ALU.mult, ALU.add), inc="eV")
        em.do(V_, lambda e: e.tensor_scalar(
            cw[:, 0:CS], cw[:, 0:CS], convb_sb[:, cc_conv:cc_conv + 1], None,
            ALU.add), inc="eV")
        em.wait(S_, "eV")
        em.do(S_, lambda e: e.activation(
            xbcw2[cc_conv % 2][:, 0:CS], cw[:, 0:CS], AF.Exp, scale=-1.0),
            inc="eS")
        em.wait(V_, "eS")
        em.do(V_, lambda e: e.tensor_scalar(
            xbcw2[cc_conv % 2][:, 0:CS], xbcw2[cc_conv % 2][:, 0:CS], 1.0,
            None, ALU.add), inc="eV")
        em.do(V_, lambda e: e.reciprocal(
            xbcw2[cc_conv % 2][:, 0:CS], xbcw2[cc_conv % 2][:, 0:CS]),
            inc="eV")
        ev = em.do(V_, lambda e: e.tensor_mul(
            dest_fn(), cw[:, 0:CS], xbcw2[cc_conv % 2][:, 0:CS]), inc="eV")
        return ("eV", ev)

    yspill = {}

    def ssd_pair(c, m):
        hsb = hsband[m % 2]
        g = m // 8
        mg = c * 64 + m
        em.wait(T_, "eV")
        em.wait(T_, "eS")
        for jh in range(2):
            em.do(T_, lambda e, jh=jh: e.transpose(
                psr[7][:, 128 * jh:128 * jh + 128],
                hsb[:, 128 * jh:128 * jh + 128], ident[:]), inc="eT")
        em.wait(V_, "eT")
        for jh in range(2):
            em.do(V_, lambda e, jh=jh: e.tensor_copy(
                xch[jh][:], psr[7][:, 128 * jh:128 * jh + 128]), inc="eV")
        ysta_t = ysta[m % 2]
        em.wait(V_, f"dYS{m % 2}", yspill.get(mg - 2, 0))
        if m < 2:
            em.wait(V_, f"dGS{m % 2}", gspill.get((c - 1, 62 + m), 0))
        if SSDSUB < 2:
            return
        for hh in (0, 1):
            h = 2 * m + hh
            hg = h % 8
            p0 = 64 * hh
            tp = None if hh == 0 else (0, 64)
            em.wait(T_, "eV")
            em.wait(T_, "eS")
            em.do(T_, lambda e, hg=hg: e.matmul(
                psr[4][:, 0:CS], ones1[0:1, :],
                cumrow[0:1, CS * hg:CS * hg + CS], start=True, stop=True),
                inc="eT")
            em.wait(S_, "eT")
            ecbm = em.do(S_, lambda e, p0=p0: e.activation(
                ecb[p0:p0 + 64, :], psr[4][p0:p0 + 64, 0:CS], AF.Exp), inc="eS")
            em.wait(V_, "eT")
            em.wait(V_, "eS", ecbm)
            em.do(V_, lambda e: e.tensor_tensor(
                d0w[:], psr[4][:, 0:CS], mask[:], ALU.add), inc="eV")
            em.do(V_, lambda e: e.tensor_tensor(
                d1w[:], psr[4][:, 128:CS], mask[:, 0:128], ALU.add), inc="eV")
            em.wait(S_, "eV")
            em.do(S_, lambda e, h=h: e.activation(
                L0w[:], d0w[:], AF.Exp, bias=ncm_tm[:, h:h + 1]), inc="eS")
            em.do(S_, lambda e, h=h: e.activation(
                L1w[:], d1w[:], AF.Exp, bias=ncm_tm[:, 128 + h:129 + h]),
                inc="eS")
            em.wait(V_, "eS")
            em.do(V_, lambda e, h=h, g=g: e.scalar_tensor_tensor(
                sT0[:], L0w[:], dt_tm[:, h:h + 1],
                rawsb[:, 384 * g:384 * g + 256], ALU.mult, ALU.mult), inc="eV")
            em.do(V_, lambda e, h=h, g=g: e.scalar_tensor_tensor(
                sT1[:], L1w[:], dt_tm[:, 128 + h:129 + h],
                rawsb[:, 384 * g + 256:384 * g + 384], ALU.mult, ALU.mult),
                inc="eV")
            if SSDSUB < 3:
                continue
            em.wait(T_, "eV")
            em.do(T_, lambda e, p0=p0, tp=tp: e.matmul(
                psr[5][p0:p0 + 64, 0:CS], xch[0][:, p0:p0 + 64], sT0[:],
                start=True, stop=False, tile_position=tp), inc="eT")
            em.do(T_, lambda e, p0=p0, tp=tp: e.matmul(
                psr[5][p0:p0 + 64, 128:CS], xch[1][:, p0:p0 + 64], sT1[:],
                start=False, stop=True, tile_position=tp), inc="eT")
            if SSDSUB < 4:
                continue
            em.do(T_, lambda e, h=h, g=g, p0=p0, tp=tp: e.matmul(
                psr[6][p0:p0 + 64, 0:CS], slocal[:, 64 * h:64 * h + 64],
                BCfm[:, COFF + CS * g:COFF + CS * g + CS],
                start=True, stop=True, tile_position=tp), inc="eT")
            em.wait(V_, "eT")
            em.do(V_, lambda e, p0=p0: e.tensor_tensor(
                t1w[p0:p0 + 64, 0:CS], psr[6][p0:p0 + 64, 0:CS],
                ecb[p0:p0 + 64, :], ALU.mult), inc="eV")
            em.do(V_, lambda e, p0=p0: e.tensor_tensor(
                t1w[p0:p0 + 64, 0:CS], t1w[p0:p0 + 64, 0:CS],
                psr[5][p0:p0 + 64, 0:CS], ALU.add), inc="eV")
            em.do(V_, lambda e, p0=p0, h=h, ysta_t=ysta_t: e.scalar_tensor_tensor(
                ysta_t[p0:p0 + 64, :], hsb[p0:p0 + 64, :],
                D_b[p0:p0 + 64, h:h + 1], t1w[p0:p0 + 64, 0:CS],
                ALU.mult, ALU.add), inc="eV")
            if SSDSUB < 5:
                continue
            em.do(V_, lambda e, h=h, g=g: e.tensor_scalar(
                bp0[:], Btm[:, CS * g:CS * g + 128], ddl_tm[:, h:h + 1], None,
                ALU.mult), inc="eV")
            em.do(V_, lambda e, h=h, g=g: e.tensor_scalar(
                bp1[:], Btm[:, CS * g + 128:CS * g + CS],
                ddl_tm[:, 128 + h:129 + h], None, ALU.mult), inc="eV")
            em.wait(T_, "eV")
            em.do(T_, lambda e, p0=p0: e.matmul(
                psr[7][:, 0:64], bp0[:], xch[0][:, p0:p0 + 64],
                start=True, stop=False), inc="eT")
            em.do(T_, lambda e, p0=p0: e.matmul(
                psr[7][:, 0:64], bp1[:], xch[1][:, p0:p0 + 64],
                start=False, stop=True), inc="eT")
            em.wait(V_, "eT")
            em.do(V_, lambda e, h=h: e.scalar_tensor_tensor(
                slocal[:, 64 * h:64 * h + 64], slocal[:, 64 * h:64 * h + 64],
                cdec_b[:, h:h + 1], psr[7][:, 0:64], ALU.mult, ALU.add),
                inc="eV")
        if SSDSUB < 6:
            return
        em.wait(G_, "eV")
        yspill[mg] = em.do(G_, lambda e, c=c, m=m, t=ysta_t: e.dma_start(
            out=ydram[c, m], in_=t[:]), inc=(f"dYS{m % 2}", 16))

    def stream_mm(Wsrc, nslabs, rhs_of_k, Wt, nout, evac_fn):
        nk = nslabs * 16
        slabs = [(i, q) for i in range(nout) for q in range(nslabs)]
        dma_at, half_marks, evac_marks = {}, {}, {}
        em.wait(Y_, "eT")
        em.wait(T_, "eV")

        def do_mms(j):
            i, q = slabs[j]
            bank = i % 4
            if q == 0 and i >= 4:
                eng_s, val = evac_marks[i - 4]
                em.wait(T_, eng_s, val)
            em.wait(T_, f"dH{j % 3}", dma_at[j])
            mmk = None
            for s in range(16):
                k = q * 16 + s
                mmk = em.do(T_, lambda e, j=j, s=s, k=k, bank=bank: e.matmul(
                    psr[bank][:, 0:Wt], hb[j % 3][:, 128 * s:128 * s + 128],
                    rhs_of_k(k), start=(k == 0), stop=(k == nk - 1)), inc="eT")
            half_marks[j] = mmk
            if q == nslabs - 1:
                evac_marks[i] = evac_fn(i, bank, mmk)

        for j, (i, q) in enumerate(slabs):
            if j >= 3:
                em.wait(Y_, "eT", half_marks[j - 3])
            dma_at[j] = em.do(Y_, lambda e, i=i, q=q, buf=j % 3: e.dma_start(
                out=hb[buf][:], in_=Wsrc[i, q].rearrange("p s c -> p (s c)")),
                inc=(f"dH{j % 3}", 16))
            if j >= 1:
                do_mms(j - 1)
        do_mms(len(slabs) - 1)

    gspill = {}
    xmspill = {}
    gcspill = {}
    cspill = {}
    for tile in range(NCH):
        Wt = W0 if tile == 0 else CS
        off = HALO if tile == 0 else 0
        col0 = 0 if tile == 0 else HALO + CS * tile
        zsrc = lambda dc, Wt=Wt: zb[:, dc, 0:Wt]
        xsrc = lambda dc, Wt=Wt: xb[:, dc, 0:Wt]

        em.wait(Y_, "eV")
        em.wait(Y_, "eT")
        em.wait(Y_, "dXM", xmspill.get(tile - 1, 0))
        xle = em.do(Y_, lambda e, col0=col0, Wt=Wt: e.dma_start(
            out=xb[:, :, 0:Wt],
            in_=x_fm.rearrange("(c p) t -> p c t", p=128)[:, :, col0:col0 + Wt]),
            inc=("dX", 16))
        em.wait(T_, "dX", xle)
        em.wait(S_, "dX", xle)
        em.wait(V_, "dX", xle)

        if SUB >= 1:
            ln_layer(xsrc, Wt, lambda dc, Wt=Wt: zb[:, dc, 0:Wt])
        if SUB >= 2:
            ffn(Wt, zsrc, lambda oc, Wt=Wt: xb[:, oc, 0:Wt])
        if SUB >= 3:
            em.wait(G_, "eV")
            xmspill[tile] = em.do(G_, lambda e, tile=tile, off=off: e.dma_start(
                out=xmdram[tile].rearrange("c p t -> p c t"),
                in_=xb[:, :, off:off + CS]), inc=("dXM", 16))
            ln_layer(xsrc, Wt, lambda dc, Wt=Wt: zb[:, dc, 0:Wt])
        if SUB < 4:
            continue

        cc_order = [144] + list(range(128, 144)) + list(range(64, 128)) \
            + list(range(0, 64))
        if SUB == 4:
            cc_order = [144]
        elif SUB == 5:
            cc_order = [144] + list(range(128, 144))
        elif SUB == 6:
            cc_order = [144] + list(range(128, 144)) + list(range(64, 128))
        dma_at, half_marks, evac_done = {}, {}, {}
        em.wait(T_, "eV")
        em.wait(T_, "eS")
        em.wait(Y_, "eT")
        for idx, cc in enumerate(cc_order):
            for q in range(2):
                j = idx * 2 + q
                if j >= 3:
                    em.wait(Y_, "eT", half_marks[j - 3])
                dma_at[j] = em.do(Y_, lambda e, cc=cc, q=q, buf=j % 3: e.dma_start(
                    out=hb[buf][:], in_=Wp[cc, q].rearrange("p s c -> p (s c)")),
                    inc=(f"dH{j % 3}", 16))
            bank = idx % 4
            if idx >= 4 and (idx - 4) in evac_done:
                eng_s, val = evac_done[idx - 4]
                em.wait(T_, eng_s, val)
            mmk = None
            for q in range(2):
                j = idx * 2 + q
                em.wait(T_, f"dH{j % 3}", dma_at[j])
                for s in range(16):
                    k = q * 16 + s
                    mmk = em.do(T_, lambda e, j=j, s=s, k=k, bank=bank, Wt=Wt:
                                e.matmul(psr[bank][:, 0:Wt],
                                         hb[j % 3][:, 128 * s:128 * s + 128],
                                         zb[:, k, 0:Wt],
                                         start=(k == 0), stop=(k == 31)),
                                inc="eT")
                half_marks[j] = mmk

            if cc == 144:
                em.wait(V_, "eT", mmk)
                ev = em.do(V_, lambda e, bank=bank, off=off: e.tensor_copy(
                    dtw[:], psr[bank][:, off:off + CS]), inc="eV")
                evac_done[idx] = ("eV", ev)
                em.do(V_, lambda e: e.tensor_scalar(
                    dtw[:], dtw[:], dtb_sb[:], None, ALU.add), inc="eV")
                em.do(V_, lambda e: e.scalar_tensor_tensor(
                    dtw2[:], dtw[:], -1.0, dtw[:], ALU.mult, ALU.max), inc="eV")
                em.wait(S_, "eV")
                em.do(S_, lambda e: e.activation(
                    dtw2[:], dtw2[:], AF.Exp, scale=-1.0), inc="eS")
                em.do(S_, lambda e: e.activation(
                    dtw2[:], dtw2[:], AF.Ln, bias=1.0), inc="eS")
                em.do(V_, lambda e: e.tensor_scalar(
                    dtw[:], dtw[:], 0.0, None, ALU.max), inc="eV")
                em.wait(V_, "eS")
                em.do(V_, lambda e: e.tensor_add(dt_fm[:], dtw[:], dtw2[:]),
                      inc="eV")
                em.do(V_, lambda e: e.tensor_scalar(
                    dtw[:], dt_fm[:], negA[:], None, ALU.mult), inc="eV")
                em.do(V_, lambda e: e.tensor_tensor_scan(
                    cum_fm[:], dtw[:], dtw[:], 0.0, ALU.add, ALU.bypass),
                    inc="eV")
                em.wait(S_, "eV")
                em.wait(S_, "dGC", gcspill.get(tile - 1, 0))
                em.do(S_, lambda e: e.activation(
                    dlast[:], cum_fm[:], AF.Exp, bias=cum_fm[:, CS - 1:CS],
                    scale=-1.0), inc="eS")
                em.do(S_, lambda e: e.activation(
                    cdec[:], cum_fm[:, CS - 1:CS], AF.Exp), inc="eS")
                em.do(S_, lambda e: e.activation(
                    gce_fm[:], cum_fm[:], AF.Exp, bias=R_sb[:]), inc="eS")
                em.wait(V_, "eS")
                em.do(V_, lambda e: e.tensor_mul(ddl_fm[:], dt_fm[:], dlast[:]),
                      inc="eV")
                em.do(V_, lambda e: e.tensor_tensor(
                    R_sb[:], R_sb[:], cum_fm[:, CS - 1:CS], ALU.add), inc="eV")
                em.wait(G_, "eS")
                gcspill[tile] = em.do(G_, lambda e, tile=tile: e.dma_start(
                    out=gcdram[tile], in_=gce_fm[:]), inc=("dGC", 16))
                em.wait(T_, "eV")
                for srcv, dstv, scale in ((dt_fm, dt_tm, None),
                                          (ddl_fm, ddl_tm, None),
                                          (cum_fm, ncm_tm, -1.0)):
                    for jh in range(2):
                        em.do(T_, lambda e, srcv=srcv, jh=jh: e.transpose(
                            psr[7][:, 128 * jh:128 * jh + 128],
                            srcv[:, 128 * jh:128 * jh + 128], ident[:]),
                            inc="eT")
                    if scale is None:
                        em.wait(V_, "eT")
                        em.do(V_, lambda e, dstv=dstv: e.tensor_copy(
                            dstv[:], psr[7][:, 0:CS]), inc="eV")
                        em.wait(T_, "eV")
                    else:
                        em.wait(S_, "eT")
                        em.do(S_, lambda e, dstv=dstv, scale=scale: e.activation(
                            dstv[:], psr[7][:, 0:CS], AF.Copy, scale=scale),
                            inc="eS")
                        em.wait(T_, "eS")
                em.do(T_, lambda e: e.transpose(
                    psr[7][0:1, 0:128], cdec[:], ident[:]), inc="eT")
                em.wait(V_, "eT")
                em.do(V_, lambda e: e.tensor_copy(
                    cdecrow[0:1, :], psr[7][0:1, 0:128]), inc="eV")
                em.wait(T_, "eV")
                em.do(T_, lambda e: e.matmul(
                    psr[7][:, 0:128], ones1[0:1, :], cdecrow[0:1, :],
                    start=True, stop=True), inc="eT")
                em.wait(V_, "eT")
                em.do(V_, lambda e: e.tensor_copy(cdec_b[:], psr[7][:, 0:128]),
                      inc="eV")
                em.wait(T_, "eV")
            elif 128 <= cc < 144:
                gch = cc - 128
                dst = BCfm[:, BOFF + CS * gch:BOFF + CS * gch + CS] if gch < 8 \
                    else BCfm[:, COFF + CS * (gch - 8):COFF + CS * (gch - 8) + CS]
                em.wait(S_, "dCS", cspill.get(tile - 1, 0))
                evac_done[idx] = conv_and_dest(
                    cc - 64, bank, mmk, Wt, off, tile, lambda dst=dst: dst)
                if cc == 143:
                    em.wait(G_, "eS")
                    em.wait(G_, "eV")
                    cspill[tile] = em.do(G_, lambda e, tile=tile: e.dma_start(
                        out=cdram[tile].rearrange("g p t -> p g t"),
                        in_=BCfm[:, COFF:COFF + G * CS].rearrange(
                            "p (g t) -> p g t", g=G)), inc=("dCS", 16))
                    em.wait(T_, "eS")
                    em.wait(T_, "eV")
                    for gg in range(G):
                        bsl = BCfm[:, BOFF + CS * gg:BOFF + CS * gg + CS]
                        csl = BCfm[:, COFF + CS * gg:COFF + CS * gg + CS]
                        em.do(T_, lambda e, bsl=bsl, csl=csl: e.matmul(
                            psr[5][:, 0:CS], bsl[:, 0:128], csl,
                            start=True, stop=True), inc="eT")
                        em.do(T_, lambda e, bsl=bsl, csl=csl: e.matmul(
                            psr[5][:, CS:CS + 128], bsl[:, 128:CS],
                            csl[:, 128:CS], start=True, stop=True), inc="eT")
                        em.wait(V_, "eT")
                        em.do(V_, lambda e, gg=gg: e.tensor_copy(
                            rawsb[:, 384 * gg:384 * gg + 384], psr[5][:, 0:384]),
                            inc="eV")
                        em.wait(T_, "eV")
                        for jh in range(2):
                            em.do(T_, lambda e, bsl=bsl, jh=jh: e.transpose(
                                psr[6][:, 128 * jh:128 * jh + 128],
                                bsl[:, 128 * jh:128 * jh + 128], ident[:]),
                                inc="eT")
                        em.wait(V_, "eT")
                        em.do(V_, lambda e, gg=gg: e.tensor_copy(
                            Btm[:, CS * gg:CS * gg + CS], psr[6][:, 0:CS]),
                            inc="eV")
                        em.wait(T_, "eV")
            elif 64 <= cc < 128:
                mpair = cc - 64
                if mpair % 4 == 0:
                    h8 = mpair // 4
                    em.wait(G_, "eT")
                    em.wait(G_, "eV")
                    cr = em.do(G_, lambda e, h8=h8: e.dma_start(
                        out=cumrow[0:1, :],
                        in_=cum_fm[8 * h8:8 * h8 + 8, :]), inc=("dFL", 16))
                    em.wait(T_, "dFL", cr)
                evac_done[idx] = conv_and_dest(
                    cc - 64, bank, mmk, Wt, off, tile,
                    lambda mpair=mpair: hsband[mpair % 2][:])
                em.wait(T_, "eV", evac_done[idx][1])
                ssd_pair(tile, mpair)
            else:
                gmk = gspill.get((tile, cc - 2))
                if gmk:
                    em.wait(S_, f"dGS{cc % 2}", gmk)
                if cc < 2:
                    em.wait(S_, f"dYS{cc % 2}", yspill.get(tile * 64 + 62 + cc, 0))
                em.wait(S_, "eT", mmk)
                ev = em.do(S_, lambda e, bank=bank, off=off, cc=cc: e.activation(
                    gsta[cc % 2][:], psr[bank][:, off:off + CS], AF.Silu),
                    inc="eS")
                evac_done[idx] = ("eS", ev)
                em.wait(G_, "eS", ev)
                gspill[(tile, cc)] = em.do(
                    G_, lambda e, tile=tile, cc=cc: e.dma_start(
                        out=gdram[tile, cc], in_=gsta[cc % 2][:]),
                    inc=(f"dGS{cc % 2}", 16))

    if PHASE < 1:
        for sname in ("dS", "dX", "dH0", "dH1", "dH2", "dYS0", "dYS1", "dGS0",
                      "dGS1", "dXM", "dCS", "dGC", "dFL", "dL2", "dXL", "dSJ",
                      "dCC", "dO"):
            em.wait(Y_, sname)
        em.wait(Y_, "eV")
        em.wait(Y_, "eT")
        em.wait(Y_, "eS")
        return em, es
    em.wait(G_, "eV")
    em.do(G_, lambda e: e.dma_start(out=ccin[:, 0:INTER], in_=slocal[:]),
          inc=("dCC", 16))
    em.do(G_, lambda e: e.dma_start(out=ccin[:, INTER:INTER + 1], in_=R_sb[:],
                                    allow_slow_non_contiguous=True),
          inc=("dCC", 16))
    em.do(G_, lambda e: e.dma_start(out=ccin[:, INTER + 1:INTER + 64],
                                    in_=slocal[:, 0:63]), inc=("dCC", 16))
    em.wait(G_, "dCC")
    em.do(G_, lambda e: e.collective_compute(
        "AllGather", ALU.bypass, replica_groups=GRP_RG,
        ins=[ccin.opt()], outs=[ccout.opt()]), inc="cc")
    em.wait(G_, "cc")
    for j in range(4):
        em.do(G_, lambda e, j=j: e.dma_start(
            out=Lcols[:, j:j + 1], in_=ccout[j][:, INTER:INTER + 1],
            allow_slow_non_contiguous=True), inc=("dL2", 16))
    em.wait(V_, "dL2")
    em.wait(V_, "eS")
    for j in range(4):
        em.do(V_, lambda e, j=j: e.tensor_scalar(
            coefw[:, 0:1], Lcols[:, 0:1], selm_b[:, 4 * j:4 * j + 1], None,
            ALU.mult), inc="eV")
        for mm_ in range(1, 4):
            em.do(V_, lambda e, j=j, mm_=mm_: e.scalar_tensor_tensor(
                coefw[:, 0:1], Lcols[:, mm_:mm_ + 1],
                selm_b[:, 4 * j + mm_:4 * j + mm_ + 1], coefw[:, 0:1],
                ALU.mult, ALU.add), inc="eV")
        em.wait(S_, "eV")
        em.do(S_, lambda e: e.activation(coefw[:, 1:2], coefw[:, 0:1], AF.Exp),
              inc="eS")
        em.wait(V_, "eS")
        em.do(V_, lambda e, j=j: e.tensor_scalar(
            coefcols[:, j:j + 1], coefw[:, 1:2], jm_b[:, j:j + 1], None,
            ALU.mult), inc="eV")
    em.wait(G_, "eV")
    em.do(G_, lambda e: e.dma_start(out=coefrow[0:1, :], in_=coefcols[:]),
          inc=("dFL", 16))
    em.wait(T_, "dFL")
    em.wait(T_, "eV")
    for j in range(4):
        em.do(T_, lambda e, j=j: e.matmul(
            psr[4][:, 0:128], ones1[0:1, :],
            coefrow[0:1, :].rearrange("o (h f) -> o h f", f=4)[:, :, j],
            start=True, stop=True), inc="eT")
        em.wait(V_, "eT")
        em.do(V_, lambda e, j=j: e.tensor_copy(
            coefb[:, 128 * j:128 * j + 128], psr[4][:, 0:128]), inc="eV")
        em.wait(T_, "eV")
    em.do(V_, lambda e: e.memset(slocal[:], 0.0), inc="eV")
    for j in range(4):
        for qq in range(4):
            em.wait(G_, "eV")
            ld = em.do(G_, lambda e, j=j, qq=qq: e.dma_start(
                out=sjq[qq % 2][:],
                in_=ccout[j][:, 2048 * qq:2048 * qq + 2048]), inc=("dSJ", 16))
            em.wait(V_, "dSJ", ld)
            for hh in range(32):
                h = 32 * qq + hh
                em.do(V_, lambda e, j=j, qq=qq, hh=hh, h=h: e.scalar_tensor_tensor(
                    slocal[:, 64 * h:64 * h + 64],
                    sjq[qq % 2][:, 64 * hh:64 * hh + 64],
                    coefb[:, 128 * j + h:128 * j + h + 1],
                    slocal[:, 64 * h:64 * h + 64], ALU.mult, ALU.add), inc="eV")

    if PHASE < 2:
        for sname in ("dS", "dX", "dH0", "dH1", "dH2", "dYS0", "dYS1", "dGS0",
                      "dGS1", "dXM", "dCS", "dGC", "dFL", "dL2", "dXL", "dSJ",
                      "dCC", "dO"):
            em.wait(Y_, sname)
        em.wait(Y_, "eV")
        em.wait(Y_, "eT")
        em.wait(Y_, "eS")
        em.wait(Y_, "cc")
        return em, es
    for sname in ("dYS0", "dYS1", "dGS0", "dGS1", "dCS", "dGC", "dXM"):
        em.wait(G_, sname)
    for c in range(NCH):
        for g in range(G):
            em.wait(G_, "eV")
            em.wait(G_, "eT")
            em.do(G_, lambda e, c=c, g=g: e.dma_start(
                out=ybuf[:].rearrange("p (k t) -> p k t", k=8),
                in_=ydram[c, 8 * g:8 * g + 8].rearrange("k p t -> p k t")),
                inc=("dL2", 16))
            em.do(G_, lambda e, c=c, g=g: e.dma_start(
                out=gbuf[:].rearrange("p (k t) -> p k t", k=8),
                in_=gdram[c, 8 * g:8 * g + 8].rearrange("k p t -> p k t")),
                inc=("dL2", 16))
            em.do(G_, lambda e, c=c, g=g: e.dma_start(
                out=csb2[:], in_=cdram[c, g]), inc=("dL2", 16))
            em.wait(T_, "dL2")
            em.wait(V_, "dL2")
            for mp in range(8):
                if mp % 4 == 0:
                    em.wait(G_, "eT")
                    ld2 = em.do(G_, lambda e, c=c, g=g, mp=mp: e.dma_start(
                        out=gcrow[0:1, :],
                        in_=gcdram[c, 16 * g + 2 * mp:16 * g + 2 * mp + 8, :]),
                        inc=("dFL", 16))
                    em.wait(T_, "dFL", ld2)
                em.wait(T_, "eV")
                for hh in (0, 1):
                    h = 16 * g + 2 * mp + hh
                    hg = (2 * mp + hh) % 8
                    p0 = 64 * hh
                    tp = None if hh == 0 else (0, 64)
                    em.do(T_, lambda e, hg=hg, p0=p0, tp=tp: e.matmul(
                        psr[4][p0:p0 + 64, 0:CS], ones1[0:1, 0:64],
                        gcrow[0:1, CS * hg:CS * hg + CS], start=True,
                        stop=True, tile_position=tp), inc="eT")
                    em.do(T_, lambda e, h=h, p0=p0, tp=tp: e.matmul(
                        psr[5][p0:p0 + 64, 0:CS], slocal[:, 64 * h:64 * h + 64],
                        csb2[:], start=True, stop=True, tile_position=tp),
                        inc="eT")
                em.wait(V_, "eT")
                em.do(V_, lambda e: e.tensor_copy(gbsb[:], psr[4][:, 0:CS]),
                      inc="eV")
                em.do(V_, lambda e: e.tensor_tensor(
                    t1w[:, 0:CS], psr[5][:, 0:CS], gbsb[:], ALU.mult), inc="eV")
                em.do(V_, lambda e, mp=mp: e.tensor_tensor(
                    t1w[:, 0:CS], t1w[:, 0:CS], ybuf[:, CS * mp:CS * mp + CS],
                    ALU.add), inc="eV")
                em.do(V_, lambda e, mp=mp: e.tensor_tensor(
                    ybuf[:, CS * mp:CS * mp + CS], t1w[:, 0:CS],
                    gbuf[:, CS * mp:CS * mp + CS], ALU.mult), inc="eV")
            sqm = {}
            em.wait(T_, "eV")
            em.wait(S_, "eV")
            em.wait(S_, "eT")
            for mp in range(8):
                if mp >= 2:
                    em.wait(S_, "eT", sqm[mp - 2])
                em.do(S_, lambda e, mp=mp: e.activation(
                    sqw[mp % 2][:, 0:CS], ybuf[:, CS * mp:CS * mp + CS],
                    AF.Square), inc="eS")
                em.wait(T_, "eS")
                sqm[mp] = em.do(T_, lambda e, mp=mp: e.matmul(
                    psr[6][0:1, 0:CS], onescol[:], sqw[mp % 2][:, 0:CS],
                    start=(mp == 0), stop=(mp == 7)), inc="eT")
            em.wait(V_, "eT")
            em.do(V_, lambda e: e.tensor_scalar(
                stat[0][0:1, 0:CS], psr[6][0:1, 0:CS], 1.0 / (16 * P), None,
                ALU.mult), inc="eV")
            em.wait(S_, "eV")
            em.do(S_, lambda e: e.activation(
                stat[0][0:1, 0:CS], stat[0][0:1, 0:CS], AF.Ln,
                bias=epsb[0:1, :]), inc="eS")
            em.do(S_, lambda e: e.activation(
                stat[0][0:1, 0:CS], stat[0][0:1, 0:CS], AF.Exp, scale=-0.5),
                inc="eS")
            em.wait(T_, "eV")
            em.wait(T_, "eS")
            em.do(T_, lambda e: e.matmul(
                psr[4][:, 0:CS], ones1[0:1, :], stat[0][0:1, 0:CS],
                start=True, stop=True), inc="eT")
            em.wait(V_, "eT")
            for mp in range(8):
                kkc = 8 * g + mp
                em.do(V_, lambda e, mp=mp, kkc=kkc: e.scalar_tensor_tensor(
                    gfull[:, kkc, :], ybuf[:, CS * mp:CS * mp + CS],
                    normw_sb[:, kkc:kkc + 1], psr[4][:, 0:CS],
                    ALU.mult, ALU.mult), inc="eV")

        def oevac(i, bank, mmark, c=c):
            par = i % 2
            em.wait(G_, "eV")
            ldm = em.do(G_, lambda e, c=c, i=i, par=par: e.dma_start(
                out=xm_sb[par][:], in_=xmdram[c, i]), inc=("dXL", 16))
            em.wait(V_, "dXL", ldm)
            em.wait(V_, "eT", mmark)
            em.wait(V_, "dO")
            return ("eV", em.do(V_, lambda e, i=i, bank=bank, par=par:
                                e.tensor_tensor(xfb[:, CS * i:CS * i + CS],
                                                psr[bank][:, 0:CS],
                                                xm_sb[par][:], ALU.add),
                                inc="eV"))

        stream_mm(Wo, 4, lambda k: gfull[:, k, :], CS, OC, oevac)

        ln_layer(lambda dc: xfb[:, CS * dc:CS * dc + CS], CS,
                 lambda dc: z3[:, dc, :])
        ffn(CS, lambda dc: z3[:, dc, :],
            lambda oc: xfb[:, CS * oc:CS * oc + CS])
        em.wait(Y_, "eV")
        for oc in range(OC):
            em.do(Y_, lambda e, c=c, oc=oc: e.dma_start(
                out=odram[c, oc], in_=xfb[:, CS * oc:CS * oc + CS]),
                inc=("dO", 16))

    for sname in ("dS", "dX", "dH0", "dH1", "dH2", "dYS0", "dYS1", "dGS0",
                  "dGS1", "dXM", "dCS", "dGC", "dFL", "dL2", "dXL", "dSJ",
                  "dCC", "dO"):
        em.wait(Y_, sname)
    return em, es


def _emit(nc, em, es):
    with es:
        blk = es.enter_context(nc.Block())

        @blk.tensor
        def _(t):
            for fn in em.ops[T_]:
                fn(t)

        @blk.vector
        def _(v):
            for fn in em.ops[V_]:
                fn(v)

        @blk.scalar
        def _(s):
            for fn in em.ops[S_]:
                fn(s)

        @blk.gpsimd
        def _(g):
            for fn in em.ops[G_]:
                fn(g)

        @blk.sync
        def _(y):
            for fn in em.ops[Y_]:
                fn(y)



def _prep_shared(i):
    out = {}
    W = np.ascontiguousarray(np.asarray(i["L0_attn_in_proj_w"], np.float32))
    out["Wp"] = np.ascontiguousarray(
        W.reshape(NCC, 128, 2, 16, 128).transpose(0, 2, 4, 3, 1))
    Wow = np.ascontiguousarray(np.asarray(i["L0_attn_out_w"], np.float32))
    out["Wo"] = np.ascontiguousarray(
        Wow.reshape(OC, 128, 4, 16, 128).transpose(0, 2, 4, 3, 1))
    A = np.asarray(i["L0_ffn_A"], np.float32)
    A2 = np.ascontiguousarray(A.reshape(128, HID).T)
    out["A_ls"] = np.ascontiguousarray(
        A2.reshape(2, 16, 128, 128).transpose(0, 2, 1, 3))
    Bm = np.asarray(i["L0_ffn_B"], np.float32)
    B2 = np.ascontiguousarray(Bm.transpose(0, 2, 1).reshape(128, HID))
    out["B_as"] = np.ascontiguousarray(
        B2.reshape(128, 2, 16, 128).transpose(1, 0, 2, 3))
    out["WrT"] = np.ascontiguousarray(np.asarray(i["L0_ffn_Wr"], np.float32).T)
    out["brv"] = np.ascontiguousarray(
        np.asarray(i["L0_ffn_br"], np.float32).reshape(1, 8))
    E = np.zeros((8, 128), np.float32)
    for j in range(8):
        E[j, 16 * j:16 * j + 16] = 1.0
    out["Emat"] = E
    out["convw"] = np.ascontiguousarray(
        np.asarray(i["L0_attn_conv_w"], np.float32)
        .reshape(80, 128, KCONV).transpose(1, 0, 2).reshape(128, 80 * KCONV))
    out["convb"] = np.ascontiguousarray(
        np.asarray(i["L0_attn_conv_b"], np.float32).reshape(80, 128).T)
    out["dtbias"] = np.ascontiguousarray(
        np.asarray(i["L0_attn_dt_bias"], np.float32).reshape(128, 1))
    out["Alog"] = np.ascontiguousarray(
        np.asarray(i["L0_attn_A_log"], np.float32).reshape(128, 1))
    out["Dvin"] = np.ascontiguousarray(
        np.asarray(i["L0_attn_D"], np.float32).reshape(1, 128))
    out["normw"] = np.ascontiguousarray(
        np.asarray(i["L0_attn_norm_w"], np.float32).reshape(KK, 128).T)
    return out


def _prep_core(x_flat, k):
    t0 = TOK * k
    if k % 4 == 0:
        halo = np.zeros((HALO, HID), np.float32)
    else:
        halo = x_flat[t0 - HALO:t0]
    xc = np.concatenate([halo, x_flat[t0:t0 + TOK]], 0)
    r = k % 4
    sel = np.zeros((1, 16), np.float32)
    jm = np.zeros((1, 4), np.float32)
    for j in range(4):
        if j < r:
            jm[0, j] = 1.0
            for m_ in range(j + 1, r):
                sel[0, 4 * j + m_] = 1.0
    return {"x_fm": np.ascontiguousarray(xc.T), "selmask": sel, "jmaskv": jm}


_NC_CACHE = {}


def _get_nc():
    if "nc" not in _NC_CACHE:
        nc = bass.Bass("TRN2", target_bir_lowering=False, num_devices=NCORES)
        em, es = build(nc)
        _emit(nc, em, es)
        _NC_CACHE["nc"] = nc
    return _NC_CACHE["nc"]


def kernel(**inputs):
    x = np.asarray(inputs["L0_attn_input_states"], np.float32)
    x_flat = np.ascontiguousarray(x.reshape(B * L, HID))
    shared = _prep_shared(inputs)
    in_maps = []
    for k in range(NCORES):
        m = dict(shared)
        m.update(_prep_core(x_flat, k))
        in_maps.append(m)
    nc = _get_nc()
    res = run_bass_kernel_spmd(nc, in_maps, core_ids=list(range(NCORES)))
    outs = []
    for k in range(NCORES):
        arr = np.asarray(res.results[k]["out"])
        outs.append(arr.transpose(0, 3, 1, 2).reshape(TOK, HID))
    return np.concatenate(outs, 0).reshape(B, L, HID)


# version 06
# speedup vs baseline: 1.2324x; 1.2324x over previous
import contextlib
import numpy as np

import concourse.bass as bass
import concourse.mybir as mybir
from concourse.bass_utils import run_bass_kernel_spmd

F32 = mybir.dt.float32
BF16 = mybir.dt.bfloat16
AF = mybir.ActivationFunctionType
ALU = mybir.AluOpType
AX = mybir.AxisListType

B, L, HID = 2, 8192, 4096
H, P, N, G, KCONV, CS = 128, 64, 128, 8, 4, 256
INTER = H * P
CONVC = INTER + 2 * G * N
PROJ = INTER + CONVC + H
EPS = 1e-5
NCORES = 8
TOK = (B * L) // NCORES
NCH = TOK // CS
HALO = 3
DC = HID // 128
NCC = PROJ // 128
OC = HID // 128
KK = INTER // 128
W0 = CS + HALO
NEG = -1e30
GRP_RG = [[0, 1, 2, 3], [4, 5, 6, 7]]
PHASE = 2
SUB = 9
SSDSUB = 9
T_, V_, S_, G_, Y_ = "tensor", "vector", "scalar", "gpsimd", "sync"


class Emitter:

    def __init__(self):
        self.ops = {k: [] for k in (T_, V_, S_, G_, Y_)}
        self.ctr = {}
        self.hwm = {}
        self.h = {}

    def add_sem(self, name, handle):
        self.h[name] = handle
        self.ctr[name] = 0

    def wait(self, eng, semname, val=None):
        if val is None:
            val = self.ctr[semname]
        if val is None or val <= 0:
            return
        key = (eng, semname)
        if self.hwm.get(key, 0) >= val:
            return
        self.hwm[key] = val
        hh = self.h[semname]
        self.ops[eng].append(lambda e, hh=hh, v=val: e.wait_ge(hh, v))

    CHAIN = {"vector": "eV", "scalar": "eS", "gpsimd": "eG"}

    def do(self, eng, fn, inc=None):
        cs = self.CHAIN.get(eng)
        if cs is not None and not (isinstance(inc, tuple)):
            self.wait(eng, cs, self.ctr[cs])
            if inc is None:
                inc = cs
        if inc is None:
            self.ops[eng].append(fn)
            return None
        name, step = (inc, 1) if isinstance(inc, str) else inc
        self.ctr[name] += step
        hh = self.h[name]

        def wrapped(e, fn=fn, hh=hh, step=step):
            fn(e).then_inc(hh, step)
        self.ops[eng].append(wrapped)
        return self.ctr[name]


def build(nc, debug=False):
    es = contextlib.ExitStack()

    dram_in = lambda n, s: nc.dram_tensor(n, s, F32, kind="ExternalInput").ap()
    x_fm = dram_in("x_fm", [HID, TOK + HALO])
    Wp = dram_in("Wp", [NCC, 2, 128, 16, 128])
    Wo = dram_in("Wo", [OC, 4, 128, 16, 128])
    A_ls = dram_in("A_ls", [2, 128, 16, 128])
    B_as = dram_in("B_as", [2, 128, 16, 128])
    WrT = dram_in("WrT", [HID, 8])
    brv = dram_in("brv", [1, 8])
    Emat = dram_in("Emat", [8, 128])
    convw = dram_in("convw", [128, 80 * KCONV])
    convb = dram_in("convb", [128, 80])
    dtbias = dram_in("dtbias", [128, 1])
    Alog = dram_in("Alog", [128, 1])
    Dvin = dram_in("Dvin", [1, 128])
    Dpair = dram_in("Dpair", [128, 64])
    normw = dram_in("normw", [128, KK])
    selmask = dram_in("selmask", [1, 16])
    jmaskv = dram_in("jmaskv", [1, 4])

    odram = nc.dram_tensor("out", [NCH, OC, 128, CS], F32, kind="ExternalOutput").ap()
    Whi_d = nc.dram_tensor("Whi_d", [NCC, 2, 128, 16, 128], BF16).ap()
    Wlo_d = nc.dram_tensor("Wlo_d", [NCC, 2, 128, 16, 128], BF16).ap()
    WOhi_d = nc.dram_tensor("WOhi_d", [OC, 4, 128, 16, 128], BF16).ap()
    WOlo_d = nc.dram_tensor("WOlo_d", [OC, 4, 128, 16, 128], BF16).ap()
    kd = dict(kind="ExternalOutput") if debug else {}
    ydram = nc.dram_tensor("ydram", [NCH, KK, 128, CS], F32, **kd).ap()
    gdram = nc.dram_tensor("gdram", [NCH, KK, 128, CS], F32, **kd).ap()
    xmdram = nc.dram_tensor("xmdram", [NCH, DC, 128, CS], F32, **kd).ap()
    cdram = nc.dram_tensor("cdram", [NCH, G, 128, CS], F32, **kd).ap()
    gcdram = nc.dram_tensor("gcdram", [NCH, 128, CS], F32, **kd).ap()
    ccin = nc.dram_tensor("ccin", [128, INTER + 64], F32, **kd).ap()
    ccout = nc.dram_tensor("ccout", [4, 128, INTER + 64], F32, **kd).ap()

    sbt = lambda n, cols, parts=128: es.enter_context(
        nc.sbuf_tensor(n, [parts, cols], F32))

    ones1 = sbt("ones1", 128, 1)
    onescol = sbt("onescol", 1)
    ident = sbt("ident", 128)
    mask = sbt("mask", CS)
    E_sb = sbt("E_sb", 128, 8)
    WrT_sb = sbt("WrT_sb", DC * 8)
    br_b = sbt("br_b", 8)
    convw_sb = sbt("convw_sb", 80 * KCONV)
    convb_sb = sbt("convb_sb", 80)
    dtb_sb = sbt("dtb_sb", 1)
    Alog_sb = sbt("Alog_sb", 1)
    negA = sbt("negA", 1)
    Drow = sbt("Drow", 128, 1)
    D_b = sbt("D_b", 128)
    Dp_sb = sbt("Dp_sb", 64)
    normw_sb = sbt("normw_sb", KK)
    brrow = sbt("brrow", 8, 1)
    smrow = sbt("smrow", 16, 1)
    jmrow = sbt("jmrow", 4, 1)
    R_sb = sbt("R_sb", 1)
    epsb = sbt("epsb", 1)
    slocal = sbt("slocal", INTER)
    Lcols = sbt("Lcols", 4)
    coefw = sbt("coefw", 2)
    selm_b = sbt("selm_b", 16)
    jm_b = sbt("jm_b", 4)
    coefcols = sbt("coefcols", 4)
    coefb = sbt("coefb", 512)
    coefrow = None

    P1 = sbt("P1", 2 * DC * W0)
    ZOFF, XOFF = 0, DC * W0
    hb = [sbt(f"hb{i}", 16 * 128) for i in range(3)]
    AR2 = sbt("AR2", DC * CS)
    xfb = AR2[:, 0:DC * CS]
    rawsb = AR2[:, 0:G * 384]
    Btm = AR2[:, 3072:3072 + G * CS]
    _o = 5120
    dtw = AR2[:, _o:_o + CS]; _o += CS
    dtw2 = AR2[:, _o:_o + CS]; _o += CS
    dt_fm = AR2[:, _o:_o + CS]; _o += CS
    cum_fm = AR2[:, _o:_o + CS]; _o += CS
    dlast = AR2[:, _o:_o + CS]; _o += CS
    ddl_fm = AR2[:, _o:_o + CS]; _o += CS
    gce_fm = AR2[:, _o:_o + CS]; _o += CS
    dt_tm = AR2[:, _o:_o + CS]; _o += CS
    ddl_tm = AR2[:, _o:_o + CS]; _o += CS
    ncm_tm = AR2[:, _o:_o + CS]; _o += CS
    hsband = [AR2[:, _o:_o + CS], AR2[:, _o + CS:_o + 2 * CS]]
    assert _o + 2 * CS == DC * CS
    sjq = [AR2[:, 0:2048], AR2[:, 2048:4096]]
    cdec = sbt("cdec", 1)
    cdecrow = sbt("cdecrow", 128, 1)
    cdec_b = sbt("cdec_b", 128)
    cumrow = sbt("cumrow", 8 * CS, 1)
    gcrow = cumrow
    coefrow = cumrow[0:1, 0:512]
    xch = [sbt(f"xch{i}", 128) for i in range(2)]
    ecb = sbt("ecb", CS)
    d1w = sbt("d1w", 128)
    L0w = sbt("L0w", CS)
    L1w = sbt("L1w", 128)
    sT0 = sbt("sT0", CS)
    sT1 = sbt("sT1", 128)
    bp0 = d1w[:, 0:128]
    bp1 = L1w[:, 0:128]
    xbcw = [sbt(f"xbcw{i}", W0) for i in range(2)]
    xbcw2 = [dtw[:, 0:CS], dtw2[:, 0:CS]]
    cvw = [sbt(f"cvw{i}", W0) for i in range(2)]
    tails = sbt("tails", 80 * HALO)
    sqw = [c[:, 0:W0] for c in cvw]
    BCfm = sbt("BCfm", 2 * G * CS)
    BOFF, COFF = 0, G * CS
    lowsb = sbt("lowsb", W0)
    d0w = lowsb[:, 0:CS]
    srow_tm = sbt("srow_tm", 8)
    srow8 = sbt("srow8", W0, 8)
    statc = sbt("statc", 4)
    stat = [sbt(f"stat{i}", W0, 1) for i in range(4)]
    ysta = [sbt(f"ysta{i}", CS) for i in range(2)]
    gsta = [y[:, 0:CS] for y in ysta]
    xm_sb = [y[:, 0:CS] for y in ysta]
    gbsb = gsta[0][:, 0:CS]
    csb2 = gsta[1][:, 0:CS]
    t1w = sbt("t1w", W0)
    sexp_sb = t1w[:, 0:W0]
    ybuf = BCfm[:, 0:8 * CS]
    gbuf = BCfm[:, 8 * CS:16 * CS]

    psr = [es.enter_context(nc.psum_tensor(f"pk{i}", [128, 512], F32))
           for i in range(8)]

    P1bf = P1[:].bitcast(BF16)
    hbbf = [h[:].bitcast(BF16) for h in hb]
    BCbf = BCfm[:].bitcast(BF16)
    XO2 = 2 * XOFF

    em = Emitter()
    for name in ("dS", "dX", "dH0", "dH1", "dH2", "dYS0", "dYS1", "dGS0",
                 "dGS1", "dXM", "dCS", "dGC", "dFL", "dL2", "dXL", "dSJ",
                 "dCC", "dO", "eT", "eV", "eS", "eG", "cc"):
        em.add_sem(name, es.enter_context(nc.semaphore(name)))

    zb = P1[:, ZOFF:ZOFF + DC * W0].rearrange("p (c t) -> p c t", c=DC)
    xb = P1[:, XOFF:XOFF + DC * W0].rearrange("p (c t) -> p c t", c=DC)
    gfull = P1[:, 0:KK * CS].rearrange("p (c t) -> p c t", c=KK)
    z3 = P1[:, XOFF:XOFF + DC * CS].rearrange("p (d t) -> p d t", d=DC)

    em.do(V_, lambda e: e.memset(ones1[:], 1.0), inc="eV")
    em.do(V_, lambda e: e.memset(onescol[:], 1.0), inc="eV")
    em.do(V_, lambda e: e.memset(R_sb[:], 0.0), inc="eV")
    em.do(V_, lambda e: e.memset(slocal[:], 0.0), inc="eV")
    em.do(V_, lambda e: e.memset(epsb[:], EPS), inc="eV")
    em.do(G_, lambda e: e.memset(ident[:], 0.0), inc="eG")
    em.wait(G_, "eG")
    em.do(G_, lambda e: e.affine_select(
        out=ident[:], in_=ident[:], compare_op=ALU.not_equal, fill=1.0,
        base=0, pattern=[[-1, 128]], channel_multiplier=1), inc="eG")
    em.do(G_, lambda e: e.memset(mask[:], 0.0), inc="eG")
    em.wait(G_, "eG")
    em.do(G_, lambda e: e.affine_select(
        out=mask[:], in_=mask[:], compare_op=ALU.is_ge, fill=NEG,
        base=0, pattern=[[1, CS]], channel_multiplier=-1), inc="eG")
    for dst, src in [
        (E_sb[:], Emat), (convw_sb[:], convw), (convb_sb[:], convb),
        (dtb_sb[:], dtbias), (normw_sb[:], normw), (Alog_sb[:], Alog),
        (Drow[:], Dvin), (brrow[:], brv), (smrow[:], selmask),
        (jmrow[:], jmaskv), (Dp_sb[:], Dpair),
    ]:
        em.do(Y_, lambda e, d=dst, s=src: e.dma_start(out=d, in_=s),
              inc=("dS", 16))
    em.do(Y_, lambda e: e.dma_start(
        out=WrT_sb[:].rearrange("p (c j) -> p c j", j=8),
        in_=WrT.rearrange("(c p) j -> p c j", p=128)), inc=("dS", 16))
    em.wait(S_, "dS")
    em.do(S_, lambda e: e.activation(negA[:], Alog_sb[:], AF.Exp), inc="eS")
    em.wait(V_, "eS")
    em.do(V_, lambda e: e.tensor_scalar(negA[:], negA[:], -1.0, None, ALU.mult),
          inc="eV")
    em.wait(T_, "dS")
    em.wait(T_, "eV")
    for dst, src, w in [(D_b, Drow, 128), (br_b, brrow, 8),
                        (selm_b, smrow, 16), (jm_b, jmrow, 4)]:
        em.do(T_, lambda e, s=src, w=w: e.matmul(
            psr[4][:, 0:w], ones1[0:1, :], s[0:1, 0:w], start=True, stop=True),
            inc="eT")
        em.wait(V_, "eT")
        em.do(V_, lambda e, d=dst, w=w: e.tensor_copy(d[:, 0:w], psr[4][:, 0:w]),
              inc="eV")
        em.wait(T_, "eV")
    em.wait(T_, "eG")
    em.wait(V_, "eG")
    em.wait(S_, "eG")

    def ln_layer(src_of_dc, Wt, dst_of_dc):
        em.wait(T_, "eV")
        em.wait(T_, "eS")
        em.wait(S_, "eT")
        em.wait(S_, "eV")
        sq_marks = {}
        for dc in range(DC):
            if dc >= 2:
                em.wait(S_, "eT", sq_marks[dc - 2])
            em.do(S_, lambda e, dc=dc: e.activation(
                sqw[dc % 2][:, 0:Wt], src_of_dc(dc), AF.Square), inc="eS")
            em.do(T_, lambda e, dc=dc: e.matmul(
                psr[4][0:1, 0:Wt], onescol[:], src_of_dc(dc),
                start=(dc == 0), stop=(dc == DC - 1)), inc="eT")
            em.wait(T_, "eS")
            sq_marks[dc] = em.do(T_, lambda e, dc=dc: e.matmul(
                psr[5][0:1, 0:Wt], onescol[:], sqw[dc % 2][:, 0:Wt],
                start=(dc == 0), stop=(dc == DC - 1)), inc="eT")
        em.wait(V_, "eT")
        em.do(V_, lambda e: e.tensor_scalar(
            stat[0][0:1, 0:Wt], psr[4][0:1, 0:Wt], 1.0 / HID, None, ALU.mult),
            inc="eV")
        em.do(V_, lambda e: e.tensor_mul(
            stat[1][0:1, 0:Wt], stat[0][0:1, 0:Wt], stat[0][0:1, 0:Wt]),
            inc="eV")
        em.do(V_, lambda e: e.scalar_tensor_tensor(
            stat[2][0:1, 0:Wt], psr[5][0:1, 0:Wt], 1.0 / HID, stat[1][0:1, 0:Wt],
            ALU.mult, ALU.subtract), inc="eV")
        em.wait(S_, "eV")
        em.do(S_, lambda e: e.activation(
            stat[3][0:1, 0:Wt], stat[2][0:1, 0:Wt], AF.Ln,
            bias=epsb[0:1, :]), inc="eS")
        em.do(S_, lambda e: e.activation(
            stat[3][0:1, 0:Wt], stat[3][0:1, 0:Wt], AF.Exp, scale=-0.5),
            inc="eS")
        em.wait(V_, "eS")
        em.do(V_, lambda e: e.scalar_tensor_tensor(
            stat[1][0:1, 0:Wt], stat[0][0:1, 0:Wt], -1.0, stat[3][0:1, 0:Wt],
            ALU.mult, ALU.mult), inc="eV")
        em.wait(T_, "eV")
        em.wait(T_, "eS")
        em.do(T_, lambda e: e.matmul(
            psr[4][:, 0:Wt], ones1[0:1, :], stat[3][0:1, 0:Wt],
            start=True, stop=True), inc="eT")
        em.do(T_, lambda e: e.matmul(
            psr[5][:, 0:Wt], ones1[0:1, :], stat[1][0:1, 0:Wt],
            start=True, stop=True), inc="eT")
        em.wait(V_, "eT")
        for dc in range(DC):
            em.do(V_, lambda e, dc=dc: e.tensor_tensor(
                dst_of_dc(dc), src_of_dc(dc), psr[4][:, 0:Wt], ALU.mult),
                inc="eV")
            em.do(V_, lambda e, dc=dc: e.tensor_tensor(
                dst_of_dc(dc), dst_of_dc(dc), psr[5][:, 0:Wt], ALU.add),
                inc="eV")

    def ffn(Wt, zsrc, res_view):
        em.wait(T_, "eV")
        em.wait(T_, "eS")
        tbs = [(i * 128, min(128, Wt - i * 128)) for i in range((Wt + 127) // 128)]
        for tb, (t0, tw) in enumerate(tbs):
            for dc in range(DC):
                em.do(T_, lambda e, tb=tb, t0=t0, tw=tw, dc=dc: e.matmul(
                    psr[6][0:tw, 8 * tb:8 * tb + 8],
                    zsrc(dc)[:, t0:t0 + tw], WrT_sb[:, 8 * dc:8 * dc + 8],
                    start=(dc == 0), stop=(dc == DC - 1)), inc="eT")
        for tb, (t0, tw) in enumerate(tbs):
            em.wait(V_, "eT")
            em.do(V_, lambda e, tb=tb, tw=tw: e.tensor_tensor(
                srow_tm[0:tw, 0:8], psr[6][0:tw, 8 * tb:8 * tb + 8],
                br_b[0:tw, :], ALU.add), inc="eV")
            em.do(V_, lambda e, tw=tw: e.reduce_max(
                statc[0:tw, 0:1], srow_tm[0:tw, 0:8], axis=AX.X, negate=True),
                inc="eV")
            em.wait(S_, "eV")
            em.do(S_, lambda e, tw=tw: e.activation(
                srow_tm[0:tw, 0:8], srow_tm[0:tw, 0:8], AF.Exp,
                bias=statc[0:tw, 0:1]), inc="eS")
            em.wait(V_, "eS")
            em.do(V_, lambda e, tw=tw: e.reduce_sum(
                statc[0:tw, 1:2], srow_tm[0:tw, 0:8], axis=AX.X), inc="eV")
            em.do(V_, lambda e, tw=tw: e.reciprocal(
                statc[0:tw, 1:2], statc[0:tw, 1:2]), inc="eV")
            em.do(V_, lambda e, tw=tw: e.tensor_scalar(
                srow_tm[0:tw, 0:8], srow_tm[0:tw, 0:8], statc[0:tw, 1:2], 8.0,
                ALU.mult, ALU.mult), inc="eV")
            em.wait(T_, "eV")
            em.do(T_, lambda e, t0=t0, tw=tw: e.transpose(
                psr[7][0:8, t0:t0 + tw], srow_tm[0:tw, 0:8], ident[0:tw, 0:tw]),
                inc="eT")
        em.wait(V_, "eT")
        em.do(V_, lambda e: e.tensor_copy(srow8[0:8, 0:Wt], psr[7][0:8, 0:Wt]),
              inc="eV")
        em.wait(Y_, "eT")
        for q in range(2):
            em.do(Y_, lambda e, q=q: e.dma_start(
                out=hb[q][:], in_=A_ls[q].rearrange("p s c -> p (s c)")),
                inc=(f"dH{q}", 16))
        em.wait(T_, "dH0")
        em.wait(T_, "dH1")
        em.wait(T_, "eV")
        for q in range(2):
            for s in range(16):
                dc = 16 * q + s
                em.do(T_, lambda e, q=q, s=s, dc=dc: e.matmul(
                    psr[0][:, 0:Wt], hb[q][:, 128 * s:128 * s + 128],
                    zsrc(dc), start=(dc == 0), stop=(dc == DC - 1)), inc="eT")
        em.do(T_, lambda e: e.matmul(
            psr[1][:, 0:Wt], E_sb[0:8, :], srow8[0:8, 0:Wt],
            start=True, stop=True), inc="eT")
        em.wait(V_, "eT")
        em.do(V_, lambda e: e.tensor_copy(sexp_sb[:, 0:Wt], psr[1][:, 0:Wt]),
              inc="eV")
        em.do(V_, lambda e: e.tensor_tensor(
            lowsb[:, 0:Wt], psr[0][:, 0:Wt], sexp_sb[:, 0:Wt], ALU.mult),
            inc="eV")
        em.wait(Y_, "eT")
        for q in range(2):
            em.do(Y_, lambda e, q=q: e.dma_start(
                out=hb[q][:], in_=B_as[q].rearrange("p s c -> p (s c)")),
                inc=(f"dH{q}", 16))
        em.wait(T_, "dH0")
        em.wait(T_, "dH1")
        em.wait(T_, "eV")
        ev_marks = {}
        for oc in range(OC):
            q, s = oc // 16, oc % 16
            bank = 2 + oc % 2
            if oc >= 2:
                em.wait(T_, "eV", ev_marks[oc - 2])
            upm = em.do(T_, lambda e, q=q, s=s, bank=bank: e.matmul(
                psr[bank][:, 0:Wt], hb[q][:, 128 * s:128 * s + 128],
                lowsb[:, 0:Wt], start=True, stop=True), inc="eT")
            em.wait(V_, "eT", upm)
            ev_marks[oc] = em.do(V_, lambda e, oc=oc, bank=bank: e.tensor_tensor(
                res_view(oc), res_view(oc), psr[bank][:, 0:Wt], ALU.add),
                inc="eV")

    def conv_and_dest(cc_conv, bank, mmk, Wt, off, tile, dest_fn):
        wtile = xbcw[cc_conv % 2]
        cw = cvw[cc_conv % 2]
        em.wait(V_, "eS")
        if tile > 0:
            em.do(V_, lambda e: e.tensor_copy(
                wtile[:, 0:HALO],
                tails[:, HALO * cc_conv:HALO * cc_conv + HALO]), inc="eV")
        em.wait(S_, "eT", mmk)
        em.wait(S_, "eV")
        emk = em.do(S_, lambda e: e.activation(
            wtile[:, HALO - off:HALO - off + Wt], psr[bank][:, 0:Wt], AF.Copy),
            inc="eS")
        em.wait(V_, "eS", emk)
        em.do(V_, lambda e: e.tensor_copy(
            tails[:, HALO * cc_conv:HALO * cc_conv + HALO],
            wtile[:, CS:CS + HALO]), inc="eV")
        em.do(V_, lambda e: e.tensor_scalar(
            cw[:, 0:CS], wtile[:, 0:CS],
            convw_sb[:, 4 * cc_conv:4 * cc_conv + 1], None, ALU.mult), inc="eV")
        for kk_ in range(1, KCONV):
            em.do(V_, lambda e, kk_=kk_: e.scalar_tensor_tensor(
                cw[:, 0:CS], wtile[:, kk_:kk_ + CS],
                convw_sb[:, 4 * cc_conv + kk_:4 * cc_conv + kk_ + 1],
                cw[:, 0:CS], ALU.mult, ALU.add), inc="eV")
        em.do(V_, lambda e: e.tensor_scalar(
            cw[:, 0:CS], cw[:, 0:CS], convb_sb[:, cc_conv:cc_conv + 1], None,
            ALU.add), inc="eV")
        em.wait(S_, "eV")
        em.do(S_, lambda e: e.activation(
            xbcw2[cc_conv % 2][:, 0:CS], cw[:, 0:CS], AF.Exp, scale=-1.0),
            inc="eS")
        em.wait(V_, "eS")
        em.do(V_, lambda e: e.tensor_scalar(
            xbcw2[cc_conv % 2][:, 0:CS], xbcw2[cc_conv % 2][:, 0:CS], 1.0,
            None, ALU.add), inc="eV")
        em.do(V_, lambda e: e.reciprocal(
            xbcw2[cc_conv % 2][:, 0:CS], xbcw2[cc_conv % 2][:, 0:CS]),
            inc="eV")
        ev = em.do(V_, lambda e: e.tensor_mul(
            dest_fn(), cw[:, 0:CS], xbcw2[cc_conv % 2][:, 0:CS]), inc="eV")
        return ("eV", ev)

    yspill = {}
    pend = {"m": None}
    marks = {"ymk0": 0, "ymk1": 0, "upd": 0, "asm1": 0, "d1": 0, "stm": 0}
    xchsets = [xch, [ddl_fm[:, 0:128], ddl_fm[:, 128:256]]]
    sTh1 = (gce_fm[:, 0:CS], dlast[:, 0:128])

    def ssd_prep(c, m, conv_mark):
        hsb = hsband[m % 2]
        g = m // 8
        xc = xchsets[m % 2]
        em.wait(T_, "eV", conv_mark)
        em.wait(T_, "eV", marks["d1"])
        for jh in range(2):
            em.do(T_, lambda e, jh=jh, hsb=hsb: e.transpose(
                psr[4 + jh][:, 256:384],
                hsb[:, 128 * jh:128 * jh + 128], ident[:]), inc="eT")
        em.wait(V_, "eT")
        xev = None
        for jh in range(2):
            xev = em.do(V_, lambda e, jh=jh, xc=xc: e.tensor_copy(
                xc[jh][:], psr[4 + jh][:, 256:384]), inc="eV")
        em.wait(T_, "eV", xev)
        b0 = em.do(T_, lambda e, hg=(2 * m) % 8: e.matmul(
            psr[4][:, 0:CS], ones1[0:1, :],
            cumrow[0:1, CS * hg:CS * hg + CS], start=True, stop=True), inc="eT")
        b1 = em.do(T_, lambda e, hg=(2 * m + 1) % 8: e.matmul(
            psr[5][:, 0:CS], ones1[0:1, :],
            cumrow[0:1, CS * hg:CS * hg + CS], start=True, stop=True), inc="eT")
        h0, h1 = 2 * m, 2 * m + 1
        em.wait(S_, "eT", b0)
        em.wait(S_, "eV", marks["asm1"])
        e0 = em.do(S_, lambda e: e.activation(
            ecb[0:64, :], psr[4][0:64, 0:CS], AF.Exp), inc="eS")
        em.wait(S_, "eT", b1)
        e1 = em.do(S_, lambda e: e.activation(
            ecb[64:128, :], psr[5][64:128, 0:CS], AF.Exp), inc="eS")
        em.wait(V_, "eS", e0)
        em.do(V_, lambda e: e.tensor_tensor(
            d0w[:], psr[4][:, 0:CS], mask[:], ALU.add), inc="eV")
        dm = em.do(V_, lambda e: e.tensor_tensor(
            d1w[:], psr[4][:, 128:CS], mask[:, 0:128], ALU.add), inc="eV")
        em.wait(S_, "eV", dm)
        em.do(S_, lambda e, h=h0: e.activation(
            L0w[:], d0w[:], AF.Exp, bias=ncm_tm[:, h:h + 1]), inc="eS")
        lm = em.do(S_, lambda e, h=h0: e.activation(
            L1w[:], d1w[:], AF.Exp, bias=ncm_tm[:, 128 + h:129 + h]), inc="eS")
        em.wait(V_, "eS", lm)
        em.wait(V_, "eT", marks["ymk0"])
        em.do(V_, lambda e, h=h0, g=g: e.scalar_tensor_tensor(
            sT0[:], L0w[:], dt_tm[:, h:h + 1],
            rawsb[:, 384 * g:384 * g + 256], ALU.mult, ALU.mult), inc="eV")
        em.do(V_, lambda e, h=h0, g=g: e.scalar_tensor_tensor(
            sT1[:], L1w[:], dt_tm[:, 128 + h:129 + h],
            rawsb[:, 384 * g + 256:384 * g + 384], ALU.mult, ALU.mult),
            inc="eV")
        em.wait(V_, "eS", e1)
        em.do(V_, lambda e: e.tensor_tensor(
            d0w[:], psr[5][:, 0:CS], mask[:], ALU.add), inc="eV")
        dm = em.do(V_, lambda e: e.tensor_tensor(
            d1w[:], psr[5][:, 128:CS], mask[:, 0:128], ALU.add), inc="eV")
        marks["d1"] = dm
        em.wait(S_, "eV", dm)
        em.do(S_, lambda e, h=h1: e.activation(
            L0w[:], d0w[:], AF.Exp, bias=ncm_tm[:, h:h + 1]), inc="eS")
        lm = em.do(S_, lambda e, h=h1: e.activation(
            L1w[:], d1w[:], AF.Exp, bias=ncm_tm[:, 128 + h:129 + h]), inc="eS")
        em.wait(V_, "eS", lm)
        em.wait(V_, "eT", marks["ymk1"])
        if m == 0:
            em.wait(V_, "dGC")
        em.do(V_, lambda e, h=h1, g=g: e.scalar_tensor_tensor(
            sTh1[0][:], L0w[:], dt_tm[:, h:h + 1],
            rawsb[:, 384 * g:384 * g + 256], ALU.mult, ALU.mult), inc="eV")
        marks["stm"] = em.do(V_, lambda e, h=h1, g=g: e.scalar_tensor_tensor(
            sTh1[1][:], L1w[:], dt_tm[:, 128 + h:129 + h],
            rawsb[:, 384 * g + 256:384 * g + 384], ALU.mult, ALU.mult),
            inc="eV")
        pend["m"] = m

    def ssd_finish(c, m):
        hsb = hsband[m % 2]
        g = m // 8
        xc = xchsets[m % 2]
        mg = c * 64 + m
        ysta_t = ysta[m % 2]
        h0, h1 = 2 * m, 2 * m + 1
        em.wait(T_, "eV", marks["stm"])
        em.wait(T_, "eV", marks["upd"])
        for hh, (a, b) in ((0, (sT0, sT1)), (1, sTh1)):
            p0 = 64 * hh
            tp = None if hh == 0 else (0, 64)
            em.do(T_, lambda e, p0=p0, tp=tp, xc=xc, a=a: e.matmul(
                psr[6][p0:p0 + 64, 0:CS], xc[0][:, p0:p0 + 64], a[:],
                start=True, stop=False, tile_position=tp), inc="eT")
            ym = em.do(T_, lambda e, p0=p0, tp=tp, xc=xc, b=b: e.matmul(
                psr[6][p0:p0 + 64, 128:CS], xc[1][:, p0:p0 + 64], b[:],
                start=False, stop=True, tile_position=tp), inc="eT")
            if hh == 0:
                marks["ymk0"] = ym
            else:
                marks["ymk1"] = ym
            em.do(T_, lambda e, h=h0 + hh, g=g, p0=p0, tp=tp: e.matmul(
                psr[7][p0:p0 + 64, 0:CS], slocal[:, 64 * h:64 * h + 64],
                BCfm[:, COFF + CS * g:COFF + CS * g + CS],
                start=True, stop=True, tile_position=tp), inc="eT")
        stm = {}
        for hh in (0, 1):
            h = h0 + hh
            em.wait(V_, "eT", stm.get(hh - 1, 0))
            em.do(V_, lambda e, h=h, g=g: e.tensor_scalar(
                bp0[:], Btm[:, CS * g:CS * g + 128], ddl_tm[:, h:h + 1], None,
                ALU.mult), inc="eV")
            bm = em.do(V_, lambda e, h=h, g=g: e.tensor_scalar(
                bp1[:], Btm[:, CS * g + 128:CS * g + CS],
                ddl_tm[:, 128 + h:129 + h], None, ALU.mult), inc="eV")
            em.wait(T_, "eV", bm)
            bank = 6 + hh
            em.do(T_, lambda e, hh=hh, bank=bank: e.matmul(
                psr[bank][:, 256:320], bp0[:], xc[0][:, 64 * hh:64 * hh + 64],
                start=True, stop=False), inc="eT")
            stm[hh] = em.do(T_, lambda e, hh=hh, bank=bank: e.matmul(
                psr[bank][:, 256:320], bp1[:], xc[1][:, 64 * hh:64 * hh + 64],
                start=False, stop=True), inc="eT")
        em.wait(V_, "eT", stm[1])
        em.wait(V_, f"dYS{m % 2}", yspill.get(mg - 2, 0))
        if m < 2:
            em.wait(V_, f"dGS{m % 2}", gspill.get((c - 1, 62 + m), 0))
        marks["asm1"] = em.do(V_, lambda e: e.tensor_tensor(
            t1w[:, 0:CS], psr[7][:, 0:CS], ecb[:], ALU.mult), inc="eV")
        em.do(V_, lambda e: e.tensor_tensor(
            t1w[:, 0:CS], t1w[:, 0:CS], psr[6][:, 0:CS], ALU.add), inc="eV")
        em.do(V_, lambda e, m=m, ysta_t=ysta_t, hsb=hsb: e.scalar_tensor_tensor(
            ysta_t[:], hsb[:], Dp_sb[:, m:m + 1], t1w[:, 0:CS],
            ALU.mult, ALU.add), inc="eV")
        em.do(V_, lambda e, h=h0: e.scalar_tensor_tensor(
            slocal[:, 64 * h:64 * h + 64], slocal[:, 64 * h:64 * h + 64],
            cdec_b[:, h:h + 1], psr[6][:, 256:320], ALU.mult, ALU.add),
            inc="eV")
        marks["upd"] = em.do(V_, lambda e, h=h1: e.scalar_tensor_tensor(
            slocal[:, 64 * h:64 * h + 64], slocal[:, 64 * h:64 * h + 64],
            cdec_b[:, h:h + 1], psr[7][:, 256:320], ALU.mult, ALU.add),
            inc="eV")
        em.wait(G_, "eV")
        yspill[mg] = em.do(G_, lambda e, c=c, m=m, t=ysta_t: e.dma_start(
            out=ydram[c, m], in_=t[:]), inc=(f"dYS{m % 2}", 16))

    def stream3(Whi, Wlo, nslabs, rhi_of_k, rlo_of_k, Wt, nout, evac_fn):
        nk = nslabs * 16
        slabs = [(i, q) for i in range(nout) for q in range(nslabs)]
        dma_at, half_marks, evac_marks = {}, {}, {}
        em.wait(Y_, "eT")
        em.wait(T_, "eV")

        def do_mms(j):
            i, q = slabs[j]
            bank = i % 4
            if q == 0 and i >= 4:
                eng_s, val = evac_marks[i - 4]
                em.wait(T_, eng_s, val)
            em.wait(T_, f"dH{j % 3}", dma_at[j])
            mmk = None
            for s in range(16):
                k = q * 16 + s
                for ps in range(3):
                    if ps == 0:
                        lhs_off, rf = 128 * s, rhi_of_k
                    elif ps == 1:
                        lhs_off, rf = 2048 + 128 * s, rhi_of_k
                    else:
                        lhs_off, rf = 128 * s, rlo_of_k
                    mmk = em.do(T_, lambda e, j=j, lhs_off=lhs_off, k=k,
                                bank=bank, rf=rf, ps=ps: e.matmul(
                                    psr[bank][:, 0:Wt],
                                    hbbf[j % 3][:, lhs_off:lhs_off + 128],
                                    rf(k), start=(k == 0 and ps == 0),
                                    stop=(k == nk - 1 and ps == 2)), inc="eT")
            half_marks[j] = mmk
            if q == nslabs - 1:
                evac_marks[i] = evac_fn(i, bank, mmk)

        for j, (i, q) in enumerate(slabs):
            if j >= 3:
                em.wait(Y_, "eT", half_marks[j - 3])
            em.do(Y_, lambda e, i=i, q=q, buf=j % 3: e.dma_start(
                out=hbbf[buf][:, 0:2048],
                in_=Whi[i, q].rearrange("p s c -> p (s c)")),
                inc=(f"dH{j % 3}", 16))
            dma_at[j] = em.do(Y_, lambda e, i=i, q=q, buf=j % 3: e.dma_start(
                out=hbbf[buf][:, 2048:4096],
                in_=Wlo[i, q].rearrange("p s c -> p (s c)")),
                inc=(f"dH{j % 3}", 16))
            if j >= 1:
                do_mms(j - 1)
        do_mms(len(slabs) - 1)

    def stream_mm(Wsrc, nslabs, rhs_of_k, Wt, nout, evac_fn):
        nk = nslabs * 16
        slabs = [(i, q) for i in range(nout) for q in range(nslabs)]
        dma_at, half_marks, evac_marks = {}, {}, {}
        em.wait(Y_, "eT")
        em.wait(T_, "eV")

        def do_mms(j):
            i, q = slabs[j]
            bank = i % 4
            if q == 0 and i >= 4:
                eng_s, val = evac_marks[i - 4]
                em.wait(T_, eng_s, val)
            em.wait(T_, f"dH{j % 3}", dma_at[j])
            mmk = None
            for s in range(16):
                k = q * 16 + s
                mmk = em.do(T_, lambda e, j=j, s=s, k=k, bank=bank: e.matmul(
                    psr[bank][:, 0:Wt], hb[j % 3][:, 128 * s:128 * s + 128],
                    rhs_of_k(k), start=(k == 0), stop=(k == nk - 1)), inc="eT")
            half_marks[j] = mmk
            if q == nslabs - 1:
                evac_marks[i] = evac_fn(i, bank, mmk)

        for j, (i, q) in enumerate(slabs):
            if j >= 3:
                em.wait(Y_, "eT", half_marks[j - 3])
            dma_at[j] = em.do(Y_, lambda e, i=i, q=q, buf=j % 3: e.dma_start(
                out=hb[buf][:], in_=Wsrc[i, q].rearrange("p s c -> p (s c)")),
                inc=(f"dH{j % 3}", 16))
            if j >= 1:
                do_mms(j - 1)
        do_mms(len(slabs) - 1)

    split_srcs = [(Wp[cc, q], Whi_d[cc, q], Wlo_d[cc, q])
                  for cc in range(NCC) for q in range(2)] + \
                 [(Wo[oc, q], WOhi_d[oc, q], WOlo_d[oc, q])
                  for oc in range(OC) for q in range(4)]
    sp_dma, sp_ev, sp_st = {}, {}, {}
    for j, (srcw, dhi, dlo) in enumerate(split_srcs):
        if j >= 3:
            em.wait(Y_, "eV", sp_ev[j - 3])
        sp_dma[j] = em.do(Y_, lambda e, srcw=srcw, buf=j % 3: e.dma_start(
            out=hb[buf][:], in_=srcw.rearrange("p s c -> p (s c)")),
            inc=(f"dH{j % 3}", 16))
        par = j % 2
        hi_st = BCbf[:, 4096 * par:4096 * par + 2048]
        lo_st = BCbf[:, 4096 * par + 2048:4096 * par + 4096]
        em.wait(V_, f"dH{j % 3}", sp_dma[j])
        em.wait(V_, f"dYS{par}", sp_st.get(j - 2, 0))
        em.do(V_, lambda e, hi_st=hi_st, buf=j % 3: e.tensor_copy(
            hi_st, hb[buf][:]), inc="eV")
        sp_ev[j] = em.do(V_, lambda e, hi_st=hi_st, lo_st=lo_st, buf=j % 3:
                         e.tensor_tensor(lo_st, hb[buf][:], hi_st,
                                         ALU.subtract), inc="eV")
        em.wait(G_, "eV", sp_ev[j])
        em.do(G_, lambda e, dhi=dhi, hi_st=hi_st: e.dma_start(
            out=dhi.rearrange("p s c -> p (s c)"), in_=hi_st),
            inc=(f"dYS{par}", 16))
        sp_st[j] = em.do(G_, lambda e, dlo=dlo, lo_st=lo_st: e.dma_start(
            out=dlo.rearrange("p s c -> p (s c)"), in_=lo_st),
            inc=(f"dYS{par}", 16))
    em.wait(Y_, "dYS0")
    em.wait(Y_, "dYS1")
    em.wait(V_, "dYS0")
    em.wait(V_, "dYS1")

    zhi = P1bf[:, XO2:XO2 + DC * W0].rearrange("p (c t) -> p c t", c=DC)
    zlo = P1bf[:, XO2 + DC * W0:XO2 + 2 * DC * W0].rearrange(
        "p (c t) -> p c t", c=DC)
    ghi = P1bf[:, 0:KK * CS].rearrange("p (c t) -> p c t", c=KK)
    glo = P1bf[:, KK * CS:2 * KK * CS].rearrange("p (c t) -> p c t", c=KK)

    gspill = {}
    xmspill = {}
    gcspill = {}
    cspill = {}
    for tile in range(NCH):
        Wt = W0 if tile == 0 else CS
        off = HALO if tile == 0 else 0
        col0 = 0 if tile == 0 else HALO + CS * tile
        zsrc = lambda dc, Wt=Wt: zb[:, dc, 0:Wt]
        xsrc = lambda dc, Wt=Wt: xb[:, dc, 0:Wt]

        em.wait(Y_, "eV")
        em.wait(Y_, "eT")
        em.wait(Y_, "dXM", xmspill.get(tile - 1, 0))
        xle = em.do(Y_, lambda e, col0=col0, Wt=Wt: e.dma_start(
            out=xb[:, :, 0:Wt],
            in_=x_fm.rearrange("(c p) t -> p c t", p=128)[:, :, col0:col0 + Wt]),
            inc=("dX", 16))
        em.wait(T_, "dX", xle)
        em.wait(S_, "dX", xle)
        em.wait(V_, "dX", xle)

        if SUB >= 1:
            ln_layer(xsrc, Wt, lambda dc, Wt=Wt: zb[:, dc, 0:Wt])
        if SUB >= 2:
            ffn(Wt, zsrc, lambda oc, Wt=Wt: xb[:, oc, 0:Wt])
        if SUB >= 3:
            em.wait(G_, "eV")
            xmspill[tile] = em.do(G_, lambda e, tile=tile, off=off: e.dma_start(
                out=xmdram[tile].rearrange("c p t -> p c t"),
                in_=xb[:, :, off:off + CS]), inc=("dXM", 16))
            ln_layer(xsrc, Wt, lambda dc, Wt=Wt: zb[:, dc, 0:Wt])
            em.wait(V_, "dXM", xmspill[tile])
            for dc in range(DC):
                em.do(V_, lambda e, dc=dc, Wt=Wt: e.tensor_copy(
                    zhi[:, dc, 0:Wt], zb[:, dc, 0:Wt]), inc="eV")
                em.do(V_, lambda e, dc=dc, Wt=Wt: e.tensor_tensor(
                    zlo[:, dc, 0:Wt], zb[:, dc, 0:Wt], zhi[:, dc, 0:Wt],
                    ALU.subtract), inc="eV")
        if SUB < 4:
            continue

        cc_order = [144] + list(range(128, 144)) + list(range(64, 128)) \
            + list(range(0, 64))
        if SUB == 4:
            cc_order = [144]
        elif SUB == 5:
            cc_order = [144] + list(range(128, 144))
        elif SUB == 6:
            cc_order = [144] + list(range(128, 144)) + list(range(64, 128))
        dma_at, half_marks, evac_done = {}, {}, {}
        em.wait(T_, "eV")
        em.wait(T_, "eS")
        em.wait(Y_, "eT")
        for idx, cc in enumerate(cc_order):
            for q in range(2):
                j = idx * 2 + q
                if j >= 3:
                    em.wait(Y_, "eT", half_marks[j - 3])
                em.do(Y_, lambda e, cc=cc, q=q, buf=j % 3: e.dma_start(
                    out=hbbf[buf][:, 0:2048],
                    in_=Whi_d[cc, q].rearrange("p s c -> p (s c)")),
                    inc=(f"dH{j % 3}", 16))
                dma_at[j] = em.do(Y_, lambda e, cc=cc, q=q, buf=j % 3: e.dma_start(
                    out=hbbf[buf][:, 2048:4096],
                    in_=Wlo_d[cc, q].rearrange("p s c -> p (s c)")),
                    inc=(f"dH{j % 3}", 16))
            bank = idx % 4
            if idx >= 4 and (idx - 4) in evac_done:
                eng_s, val = evac_done[idx - 4]
                em.wait(T_, eng_s, val)
            mmk = None
            for q in range(2):
                j = idx * 2 + q
                em.wait(T_, f"dH{j % 3}", dma_at[j])
                for s in range(16):
                    k = q * 16 + s
                    for ps in range(3):
                        lhs_off = 128 * s if ps < 2 else 2048 + 128 * s
                        rv = (zhi if ps != 1 else zlo) if ps != 2 else zhi
                        if ps == 0:
                            lhs_off, rv = 128 * s, zhi
                        elif ps == 1:
                            lhs_off, rv = 2048 + 128 * s, zhi
                        else:
                            lhs_off, rv = 128 * s, zlo
                        mmk = em.do(T_, lambda e, j=j, lhs_off=lhs_off, k=k,
                                    bank=bank, Wt=Wt, rv=rv, ps=ps:
                                    e.matmul(psr[bank][:, 0:Wt],
                                             hbbf[j % 3][:, lhs_off:lhs_off + 128],
                                             rv[:, k, 0:Wt],
                                             start=(k == 0 and ps == 0),
                                             stop=(k == 31 and ps == 2)),
                                    inc="eT")
                half_marks[j] = mmk

            if cc == 144:
                em.wait(V_, "eT", mmk)
                ev = em.do(V_, lambda e, bank=bank, off=off: e.tensor_copy(
                    dtw[:], psr[bank][:, off:off + CS]), inc="eV")
                evac_done[idx] = ("eV", ev)
                em.do(V_, lambda e: e.tensor_scalar(
                    dtw[:], dtw[:], dtb_sb[:], None, ALU.add), inc="eV")
                em.do(V_, lambda e: e.scalar_tensor_tensor(
                    dtw2[:], dtw[:], -1.0, dtw[:], ALU.mult, ALU.max), inc="eV")
                em.wait(S_, "eV")
                em.do(S_, lambda e: e.activation(
                    dtw2[:], dtw2[:], AF.Exp, scale=-1.0), inc="eS")
                em.do(S_, lambda e: e.activation(
                    dtw2[:], dtw2[:], AF.Ln, bias=1.0), inc="eS")
                em.do(V_, lambda e: e.tensor_scalar(
                    dtw[:], dtw[:], 0.0, None, ALU.max), inc="eV")
                em.wait(V_, "eS")
                em.do(V_, lambda e: e.tensor_add(dt_fm[:], dtw[:], dtw2[:]),
                      inc="eV")
                em.do(V_, lambda e: e.tensor_scalar(
                    dtw[:], dt_fm[:], negA[:], None, ALU.mult), inc="eV")
                em.do(V_, lambda e: e.tensor_tensor_scan(
                    cum_fm[:], dtw[:], dtw[:], 0.0, ALU.add, ALU.bypass),
                    inc="eV")
                em.wait(S_, "eV")
                em.wait(S_, "dGC", gcspill.get(tile - 1, 0))
                em.do(S_, lambda e: e.activation(
                    dlast[:], cum_fm[:], AF.Exp, bias=cum_fm[:, CS - 1:CS],
                    scale=-1.0), inc="eS")
                em.do(S_, lambda e: e.activation(
                    cdec[:], cum_fm[:, CS - 1:CS], AF.Exp), inc="eS")
                em.do(S_, lambda e: e.activation(
                    gce_fm[:], cum_fm[:], AF.Exp, bias=R_sb[:]), inc="eS")
                em.wait(V_, "eS")
                em.do(V_, lambda e: e.tensor_mul(ddl_fm[:], dt_fm[:], dlast[:]),
                      inc="eV")
                em.do(V_, lambda e: e.tensor_tensor(
                    R_sb[:], R_sb[:], cum_fm[:, CS - 1:CS], ALU.add), inc="eV")
                em.wait(G_, "eS")
                gcspill[tile] = em.do(G_, lambda e, tile=tile: e.dma_start(
                    out=gcdram[tile], in_=gce_fm[:]), inc=("dGC", 16))
                em.wait(T_, "eV")
                for srcv, dstv, scale in ((dt_fm, dt_tm, None),
                                          (ddl_fm, ddl_tm, None),
                                          (cum_fm, ncm_tm, -1.0)):
                    for jh in range(2):
                        em.do(T_, lambda e, srcv=srcv, jh=jh: e.transpose(
                            psr[7][:, 128 * jh:128 * jh + 128],
                            srcv[:, 128 * jh:128 * jh + 128], ident[:]),
                            inc="eT")
                    if scale is None:
                        em.wait(V_, "eT")
                        em.do(V_, lambda e, dstv=dstv: e.tensor_copy(
                            dstv[:], psr[7][:, 0:CS]), inc="eV")
                        em.wait(T_, "eV")
                    else:
                        em.wait(S_, "eT")
                        em.do(S_, lambda e, dstv=dstv, scale=scale: e.activation(
                            dstv[:], psr[7][:, 0:CS], AF.Copy, scale=scale),
                            inc="eS")
                        em.wait(T_, "eS")
                em.do(T_, lambda e: e.transpose(
                    psr[7][0:1, 0:128], cdec[:], ident[:]), inc="eT")
                em.wait(V_, "eT")
                em.do(V_, lambda e: e.tensor_copy(
                    cdecrow[0:1, :], psr[7][0:1, 0:128]), inc="eV")
                em.wait(T_, "eV")
                em.do(T_, lambda e: e.matmul(
                    psr[7][:, 0:128], ones1[0:1, :], cdecrow[0:1, :],
                    start=True, stop=True), inc="eT")
                em.wait(V_, "eT")
                em.do(V_, lambda e: e.tensor_copy(cdec_b[:], psr[7][:, 0:128]),
                      inc="eV")
                em.wait(T_, "eV")
            elif 128 <= cc < 144:
                gch = cc - 128
                dst = BCfm[:, BOFF + CS * gch:BOFF + CS * gch + CS] if gch < 8 \
                    else BCfm[:, COFF + CS * (gch - 8):COFF + CS * (gch - 8) + CS]
                em.wait(S_, "dCS", cspill.get(tile - 1, 0))
                evac_done[idx] = conv_and_dest(
                    cc - 64, bank, mmk, Wt, off, tile, lambda dst=dst: dst)
                if cc == 143:
                    em.wait(G_, "eS")
                    em.wait(G_, "eV")
                    cspill[tile] = em.do(G_, lambda e, tile=tile: e.dma_start(
                        out=cdram[tile].rearrange("g p t -> p g t"),
                        in_=BCfm[:, COFF:COFF + G * CS].rearrange(
                            "p (g t) -> p g t", g=G)), inc=("dCS", 16))
                    em.wait(T_, "eS")
                    em.wait(T_, "eV")
                    for gg in range(G):
                        bsl = BCfm[:, BOFF + CS * gg:BOFF + CS * gg + CS]
                        csl = BCfm[:, COFF + CS * gg:COFF + CS * gg + CS]
                        em.do(T_, lambda e, bsl=bsl, csl=csl: e.matmul(
                            psr[5][:, 0:CS], bsl[:, 0:128], csl,
                            start=True, stop=True), inc="eT")
                        em.do(T_, lambda e, bsl=bsl, csl=csl: e.matmul(
                            psr[5][:, CS:CS + 128], bsl[:, 128:CS],
                            csl[:, 128:CS], start=True, stop=True), inc="eT")
                        em.wait(V_, "eT")
                        em.do(V_, lambda e, gg=gg: e.tensor_copy(
                            rawsb[:, 384 * gg:384 * gg + 384], psr[5][:, 0:384]),
                            inc="eV")
                        em.wait(T_, "eV")
                        for jh in range(2):
                            em.do(T_, lambda e, bsl=bsl, jh=jh: e.transpose(
                                psr[6][:, 128 * jh:128 * jh + 128],
                                bsl[:, 128 * jh:128 * jh + 128], ident[:]),
                                inc="eT")
                        em.wait(V_, "eT")
                        em.do(V_, lambda e, gg=gg: e.tensor_copy(
                            Btm[:, CS * gg:CS * gg + CS], psr[6][:, 0:CS]),
                            inc="eV")
                        em.wait(T_, "eV")
            elif 64 <= cc < 128:
                mpair = cc - 64
                if mpair % 4 == 0:
                    h8 = mpair // 4
                    em.wait(G_, "eT")
                    em.wait(G_, "eV")
                    cr = em.do(G_, lambda e, h8=h8: e.dma_start(
                        out=cumrow[0:1, :],
                        in_=cum_fm[8 * h8:8 * h8 + 8, :]), inc=("dFL", 16))
                    em.wait(T_, "dFL", cr)
                evac_done[idx] = conv_and_dest(
                    cc - 64, bank, mmk, Wt, off, tile,
                    lambda mpair=mpair: hsband[mpair % 2][:])
                if pend["m"] is not None:
                    ssd_finish(tile, pend["m"])
                    pend["m"] = None
                ssd_prep(tile, mpair, evac_done[idx][1])
            else:
                if pend["m"] is not None:
                    ssd_finish(tile, pend["m"])
                    pend["m"] = None
                gmk = gspill.get((tile, cc - 2))
                if gmk:
                    em.wait(S_, f"dGS{cc % 2}", gmk)
                if cc < 2:
                    em.wait(S_, f"dYS{cc % 2}", yspill.get(tile * 64 + 62 + cc, 0))
                em.wait(S_, "eT", mmk)
                ev = em.do(S_, lambda e, bank=bank, off=off, cc=cc: e.activation(
                    gsta[cc % 2][:], psr[bank][:, off:off + CS], AF.Silu),
                    inc="eS")
                evac_done[idx] = ("eS", ev)
                em.wait(G_, "eS", ev)
                gspill[(tile, cc)] = em.do(
                    G_, lambda e, tile=tile, cc=cc: e.dma_start(
                        out=gdram[tile, cc], in_=gsta[cc % 2][:]),
                    inc=(f"dGS{cc % 2}", 16))

    if pend["m"] is not None:
        ssd_finish(NCH - 1, pend["m"])
        pend["m"] = None

    if PHASE < 1:
        for sname in ("dS", "dX", "dH0", "dH1", "dH2", "dYS0", "dYS1", "dGS0",
                      "dGS1", "dXM", "dCS", "dGC", "dFL", "dL2", "dXL", "dSJ",
                      "dCC", "dO"):
            em.wait(Y_, sname)
        em.wait(Y_, "eV")
        em.wait(Y_, "eT")
        em.wait(Y_, "eS")
        return em, es
    em.wait(G_, "eV")
    em.do(G_, lambda e: e.dma_start(out=ccin[:, 0:INTER], in_=slocal[:]),
          inc=("dCC", 16))
    em.do(G_, lambda e: e.dma_start(out=ccin[:, INTER:INTER + 1], in_=R_sb[:],
                                    allow_slow_non_contiguous=True),
          inc=("dCC", 16))
    em.do(G_, lambda e: e.dma_start(out=ccin[:, INTER + 1:INTER + 64],
                                    in_=slocal[:, 0:63]), inc=("dCC", 16))
    em.wait(G_, "dCC")
    em.do(G_, lambda e: e.collective_compute(
        "AllGather", ALU.bypass, replica_groups=GRP_RG,
        ins=[ccin.opt()], outs=[ccout.opt()]), inc="cc")
    em.wait(G_, "cc")
    for j in range(4):
        em.do(G_, lambda e, j=j: e.dma_start(
            out=Lcols[:, j:j + 1], in_=ccout[j][:, INTER:INTER + 1],
            allow_slow_non_contiguous=True), inc=("dL2", 16))
    em.wait(V_, "dL2")
    em.wait(V_, "eS")
    for j in range(4):
        em.do(V_, lambda e, j=j: e.tensor_scalar(
            coefw[:, 0:1], Lcols[:, 0:1], selm_b[:, 4 * j:4 * j + 1], None,
            ALU.mult), inc="eV")
        for mm_ in range(1, 4):
            em.do(V_, lambda e, j=j, mm_=mm_: e.scalar_tensor_tensor(
                coefw[:, 0:1], Lcols[:, mm_:mm_ + 1],
                selm_b[:, 4 * j + mm_:4 * j + mm_ + 1], coefw[:, 0:1],
                ALU.mult, ALU.add), inc="eV")
        em.wait(S_, "eV")
        em.do(S_, lambda e: e.activation(coefw[:, 1:2], coefw[:, 0:1], AF.Exp),
              inc="eS")
        em.wait(V_, "eS")
        em.do(V_, lambda e, j=j: e.tensor_scalar(
            coefcols[:, j:j + 1], coefw[:, 1:2], jm_b[:, j:j + 1], None,
            ALU.mult), inc="eV")
    em.wait(G_, "eV")
    em.do(G_, lambda e: e.dma_start(out=coefrow[0:1, :], in_=coefcols[:]),
          inc=("dFL", 16))
    em.wait(T_, "dFL")
    em.wait(T_, "eV")
    for j in range(4):
        em.do(T_, lambda e, j=j: e.matmul(
            psr[4][:, 0:128], ones1[0:1, :],
            coefrow[0:1, :].rearrange("o (h f) -> o h f", f=4)[:, :, j],
            start=True, stop=True), inc="eT")
        em.wait(V_, "eT")
        em.do(V_, lambda e, j=j: e.tensor_copy(
            coefb[:, 128 * j:128 * j + 128], psr[4][:, 0:128]), inc="eV")
        em.wait(T_, "eV")
    em.do(V_, lambda e: e.memset(slocal[:], 0.0), inc="eV")
    for j in range(4):
        for qq in range(4):
            em.wait(G_, "eV")
            ld = em.do(G_, lambda e, j=j, qq=qq: e.dma_start(
                out=sjq[qq % 2][:],
                in_=ccout[j][:, 2048 * qq:2048 * qq + 2048]), inc=("dSJ", 16))
            em.wait(V_, "dSJ", ld)
            for hh in range(32):
                h = 32 * qq + hh
                em.do(V_, lambda e, j=j, qq=qq, hh=hh, h=h: e.scalar_tensor_tensor(
                    slocal[:, 64 * h:64 * h + 64],
                    sjq[qq % 2][:, 64 * hh:64 * hh + 64],
                    coefb[:, 128 * j + h:128 * j + h + 1],
                    slocal[:, 64 * h:64 * h + 64], ALU.mult, ALU.add), inc="eV")

    if PHASE < 2:
        for sname in ("dS", "dX", "dH0", "dH1", "dH2", "dYS0", "dYS1", "dGS0",
                      "dGS1", "dXM", "dCS", "dGC", "dFL", "dL2", "dXL", "dSJ",
                      "dCC", "dO"):
            em.wait(Y_, sname)
        em.wait(Y_, "eV")
        em.wait(Y_, "eT")
        em.wait(Y_, "eS")
        em.wait(Y_, "cc")
        return em, es
    for sname in ("dYS0", "dYS1", "dGS0", "dGS1", "dCS", "dGC", "dXM"):
        em.wait(G_, sname)
    for c in range(NCH):
        for g in range(G):
            em.wait(G_, "eV")
            em.wait(G_, "eT")
            em.do(G_, lambda e, c=c, g=g: e.dma_start(
                out=ybuf[:].rearrange("p (k t) -> p k t", k=8),
                in_=ydram[c, 8 * g:8 * g + 8].rearrange("k p t -> p k t")),
                inc=("dL2", 16))
            em.do(G_, lambda e, c=c, g=g: e.dma_start(
                out=gbuf[:].rearrange("p (k t) -> p k t", k=8),
                in_=gdram[c, 8 * g:8 * g + 8].rearrange("k p t -> p k t")),
                inc=("dL2", 16))
            em.do(G_, lambda e, c=c, g=g: e.dma_start(
                out=csb2[:], in_=cdram[c, g]), inc=("dL2", 16))
            em.wait(T_, "dL2")
            em.wait(V_, "dL2")
            for mp in range(8):
                if mp % 4 == 0:
                    em.wait(G_, "eT")
                    ld2 = em.do(G_, lambda e, c=c, g=g, mp=mp: e.dma_start(
                        out=gcrow[0:1, :],
                        in_=gcdram[c, 16 * g + 2 * mp:16 * g + 2 * mp + 8, :]),
                        inc=("dFL", 16))
                    em.wait(T_, "dFL", ld2)
                em.wait(T_, "eV")
                for hh in (0, 1):
                    h = 16 * g + 2 * mp + hh
                    hg = (2 * mp + hh) % 8
                    p0 = 64 * hh
                    tp = None if hh == 0 else (0, 64)
                    em.do(T_, lambda e, hg=hg, p0=p0, tp=tp: e.matmul(
                        psr[4][p0:p0 + 64, 0:CS], ones1[0:1, 0:64],
                        gcrow[0:1, CS * hg:CS * hg + CS], start=True,
                        stop=True, tile_position=tp), inc="eT")
                    em.do(T_, lambda e, h=h, p0=p0, tp=tp: e.matmul(
                        psr[5][p0:p0 + 64, 0:CS], slocal[:, 64 * h:64 * h + 64],
                        csb2[:], start=True, stop=True, tile_position=tp),
                        inc="eT")
                em.wait(V_, "eT")
                em.do(V_, lambda e: e.tensor_copy(gbsb[:], psr[4][:, 0:CS]),
                      inc="eV")
                em.do(V_, lambda e: e.tensor_tensor(
                    t1w[:, 0:CS], psr[5][:, 0:CS], gbsb[:], ALU.mult), inc="eV")
                em.do(V_, lambda e, mp=mp: e.tensor_tensor(
                    t1w[:, 0:CS], t1w[:, 0:CS], ybuf[:, CS * mp:CS * mp + CS],
                    ALU.add), inc="eV")
                em.do(V_, lambda e, mp=mp: e.tensor_tensor(
                    ybuf[:, CS * mp:CS * mp + CS], t1w[:, 0:CS],
                    gbuf[:, CS * mp:CS * mp + CS], ALU.mult), inc="eV")
            sqm = {}
            em.wait(T_, "eV")
            em.wait(S_, "eV")
            em.wait(S_, "eT")
            for mp in range(8):
                if mp >= 2:
                    em.wait(S_, "eT", sqm[mp - 2])
                em.do(S_, lambda e, mp=mp: e.activation(
                    sqw[mp % 2][:, 0:CS], ybuf[:, CS * mp:CS * mp + CS],
                    AF.Square), inc="eS")
                em.wait(T_, "eS")
                sqm[mp] = em.do(T_, lambda e, mp=mp: e.matmul(
                    psr[6][0:1, 0:CS], onescol[:], sqw[mp % 2][:, 0:CS],
                    start=(mp == 0), stop=(mp == 7)), inc="eT")
            em.wait(V_, "eT")
            em.do(V_, lambda e: e.tensor_scalar(
                stat[0][0:1, 0:CS], psr[6][0:1, 0:CS], 1.0 / (16 * P), None,
                ALU.mult), inc="eV")
            em.wait(S_, "eV")
            em.do(S_, lambda e: e.activation(
                stat[0][0:1, 0:CS], stat[0][0:1, 0:CS], AF.Ln,
                bias=epsb[0:1, :]), inc="eS")
            em.do(S_, lambda e: e.activation(
                stat[0][0:1, 0:CS], stat[0][0:1, 0:CS], AF.Exp, scale=-0.5),
                inc="eS")
            em.wait(T_, "eV")
            em.wait(T_, "eS")
            em.do(T_, lambda e: e.matmul(
                psr[4][:, 0:CS], ones1[0:1, :], stat[0][0:1, 0:CS],
                start=True, stop=True), inc="eT")
            em.wait(V_, "eT")
            for mp in range(8):
                kkc = 8 * g + mp
                em.do(V_, lambda e, mp=mp, kkc=kkc: e.scalar_tensor_tensor(
                    t1w[:, 0:CS], ybuf[:, CS * mp:CS * mp + CS],
                    normw_sb[:, kkc:kkc + 1], psr[4][:, 0:CS],
                    ALU.mult, ALU.mult), inc="eV")
                em.do(V_, lambda e, kkc=kkc: e.tensor_copy(
                    ghi[:, kkc, :], t1w[:, 0:CS]), inc="eV")
                em.do(V_, lambda e, kkc=kkc: e.tensor_tensor(
                    glo[:, kkc, :], t1w[:, 0:CS], ghi[:, kkc, :],
                    ALU.subtract), inc="eV")

        def oevac(i, bank, mmark, c=c):
            par = i % 2
            em.wait(G_, "eV")
            ldm = em.do(G_, lambda e, c=c, i=i, par=par: e.dma_start(
                out=xm_sb[par][:], in_=xmdram[c, i]), inc=("dXL", 16))
            em.wait(V_, "dXL", ldm)
            em.wait(V_, "eT", mmark)
            em.wait(V_, "dO")
            return ("eV", em.do(V_, lambda e, i=i, bank=bank, par=par:
                                e.tensor_tensor(xfb[:, CS * i:CS * i + CS],
                                                psr[bank][:, 0:CS],
                                                xm_sb[par][:], ALU.add),
                                inc="eV"))

        stream3(WOhi_d, WOlo_d, 4,
                lambda k: ghi[:, k, :], lambda k: glo[:, k, :], CS, OC, oevac)

        ln_layer(lambda dc: xfb[:, CS * dc:CS * dc + CS], CS,
                 lambda dc: z3[:, dc, :])
        ffn(CS, lambda dc: z3[:, dc, :],
            lambda oc: xfb[:, CS * oc:CS * oc + CS])
        em.wait(Y_, "eV")
        for oc in range(OC):
            em.do(Y_, lambda e, c=c, oc=oc: e.dma_start(
                out=odram[c, oc], in_=xfb[:, CS * oc:CS * oc + CS]),
                inc=("dO", 16))

    for sname in ("dS", "dX", "dH0", "dH1", "dH2", "dYS0", "dYS1", "dGS0",
                  "dGS1", "dXM", "dCS", "dGC", "dFL", "dL2", "dXL", "dSJ",
                  "dCC", "dO"):
        em.wait(Y_, sname)
    return em, es


def _emit(nc, em, es):
    with es:
        blk = es.enter_context(nc.Block())

        @blk.tensor
        def _(t):
            for fn in em.ops[T_]:
                fn(t)

        @blk.vector
        def _(v):
            for fn in em.ops[V_]:
                fn(v)

        @blk.scalar
        def _(s):
            for fn in em.ops[S_]:
                fn(s)

        @blk.gpsimd
        def _(g):
            for fn in em.ops[G_]:
                fn(g)

        @blk.sync
        def _(y):
            for fn in em.ops[Y_]:
                fn(y)



def _prep_shared(i):
    out = {}
    W = np.ascontiguousarray(np.asarray(i["L0_attn_in_proj_w"], np.float32))
    out["Wp"] = np.ascontiguousarray(
        W.reshape(NCC, 128, 2, 16, 128).transpose(0, 2, 4, 3, 1))
    Wow = np.ascontiguousarray(np.asarray(i["L0_attn_out_w"], np.float32))
    out["Wo"] = np.ascontiguousarray(
        Wow.reshape(OC, 128, 4, 16, 128).transpose(0, 2, 4, 3, 1))
    A = np.asarray(i["L0_ffn_A"], np.float32)
    A2 = np.ascontiguousarray(A.reshape(128, HID).T)
    out["A_ls"] = np.ascontiguousarray(
        A2.reshape(2, 16, 128, 128).transpose(0, 2, 1, 3))
    Bm = np.asarray(i["L0_ffn_B"], np.float32)
    B2 = np.ascontiguousarray(Bm.transpose(0, 2, 1).reshape(128, HID))
    out["B_as"] = np.ascontiguousarray(
        B2.reshape(128, 2, 16, 128).transpose(1, 0, 2, 3))
    out["WrT"] = np.ascontiguousarray(np.asarray(i["L0_ffn_Wr"], np.float32).T)
    out["brv"] = np.ascontiguousarray(
        np.asarray(i["L0_ffn_br"], np.float32).reshape(1, 8))
    E = np.zeros((8, 128), np.float32)
    for j in range(8):
        E[j, 16 * j:16 * j + 16] = 1.0
    out["Emat"] = E
    out["convw"] = np.ascontiguousarray(
        np.asarray(i["L0_attn_conv_w"], np.float32)
        .reshape(80, 128, KCONV).transpose(1, 0, 2).reshape(128, 80 * KCONV))
    out["convb"] = np.ascontiguousarray(
        np.asarray(i["L0_attn_conv_b"], np.float32).reshape(80, 128).T)
    out["dtbias"] = np.ascontiguousarray(
        np.asarray(i["L0_attn_dt_bias"], np.float32).reshape(128, 1))
    out["Alog"] = np.ascontiguousarray(
        np.asarray(i["L0_attn_A_log"], np.float32).reshape(128, 1))
    Dv = np.asarray(i["L0_attn_D"], np.float32)
    out["Dvin"] = np.ascontiguousarray(Dv.reshape(1, 128))
    out["Dpair"] = np.ascontiguousarray(np.concatenate(
        [np.broadcast_to(Dv[0::2], (64, 64)), np.broadcast_to(Dv[1::2], (64, 64))],
        0))
    out["normw"] = np.ascontiguousarray(
        np.asarray(i["L0_attn_norm_w"], np.float32).reshape(KK, 128).T)
    return out


def _prep_core(x_flat, k):
    t0 = TOK * k
    if k % 4 == 0:
        halo = np.zeros((HALO, HID), np.float32)
    else:
        halo = x_flat[t0 - HALO:t0]
    xc = np.concatenate([halo, x_flat[t0:t0 + TOK]], 0)
    r = k % 4
    sel = np.zeros((1, 16), np.float32)
    jm = np.zeros((1, 4), np.float32)
    for j in range(4):
        if j < r:
            jm[0, j] = 1.0
            for m_ in range(j + 1, r):
                sel[0, 4 * j + m_] = 1.0
    return {"x_fm": np.ascontiguousarray(xc.T), "selmask": sel, "jmaskv": jm}


_NC_CACHE = {}


def _get_nc():
    if "nc" not in _NC_CACHE:
        nc = bass.Bass("TRN2", target_bir_lowering=False, num_devices=NCORES)
        em, es = build(nc)
        _emit(nc, em, es)
        _NC_CACHE["nc"] = nc
    return _NC_CACHE["nc"]


def kernel(**inputs):
    x = np.asarray(inputs["L0_attn_input_states"], np.float32)
    x_flat = np.ascontiguousarray(x.reshape(B * L, HID))
    shared = _prep_shared(inputs)
    in_maps = []
    for k in range(NCORES):
        m = dict(shared)
        m.update(_prep_core(x_flat, k))
        in_maps.append(m)
    nc = _get_nc()
    res = run_bass_kernel_spmd(nc, in_maps, core_ids=list(range(NCORES)))
    outs = []
    for k in range(NCORES):
        arr = np.asarray(res.results[k]["out"])
        outs.append(arr.transpose(0, 3, 1, 2).reshape(TOK, HID))
    return np.concatenate(outs, 0).reshape(B, L, HID)


# version 07
# speedup vs baseline: 1.2421x; 1.0079x over previous
import contextlib
import numpy as np

import concourse.bass as bass
import concourse.mybir as mybir
from concourse.bass_utils import run_bass_kernel_spmd

F32 = mybir.dt.float32
BF16 = mybir.dt.bfloat16
AF = mybir.ActivationFunctionType
ALU = mybir.AluOpType
AX = mybir.AxisListType

B, L, HID = 2, 8192, 4096
H, P, N, G, KCONV, CS = 128, 64, 128, 8, 4, 256
INTER = H * P
CONVC = INTER + 2 * G * N
PROJ = INTER + CONVC + H
EPS = 1e-5
NCORES = 8
TOK = (B * L) // NCORES
NCH = TOK // CS
HALO = 3
DC = HID // 128
NCC = PROJ // 128
OC = HID // 128
KK = INTER // 128
W0 = CS + HALO
NEG = -1e30
GRP_RG = [[0, 1, 2, 3], [4, 5, 6, 7]]
PHASE = 2
SUB = 9
SSDSUB = 9
T_, V_, S_, G_, Y_ = "tensor", "vector", "scalar", "gpsimd", "sync"


class Emitter:

    def __init__(self):
        self.ops = {k: [] for k in (T_, V_, S_, G_, Y_)}
        self.ctr = {}
        self.hwm = {}
        self.h = {}

    def add_sem(self, name, handle):
        self.h[name] = handle
        self.ctr[name] = 0

    def wait(self, eng, semname, val=None):
        if val is None:
            val = self.ctr[semname]
        if val is None or val <= 0:
            return
        key = (eng, semname)
        if self.hwm.get(key, 0) >= val:
            return
        self.hwm[key] = val
        hh = self.h[semname]
        self.ops[eng].append(lambda e, hh=hh, v=val: e.wait_ge(hh, v))

    CHAIN = {"vector": "eV", "scalar": "eS", "gpsimd": "eG"}

    def do(self, eng, fn, inc=None):
        cs = self.CHAIN.get(eng)
        if cs is not None and not (isinstance(inc, tuple)):
            self.wait(eng, cs, self.ctr[cs])
            if inc is None:
                inc = cs
        if inc is None:
            self.ops[eng].append(fn)
            return None
        name, step = (inc, 1) if isinstance(inc, str) else inc
        self.ctr[name] += step
        hh = self.h[name]

        def wrapped(e, fn=fn, hh=hh, step=step):
            fn(e).then_inc(hh, step)
        self.ops[eng].append(wrapped)
        return self.ctr[name]


def build(nc, debug=False):
    es = contextlib.ExitStack()

    dram_in = lambda n, s: nc.dram_tensor(n, s, F32, kind="ExternalInput").ap()
    x_fm = dram_in("x_fm", [HID, TOK + HALO])
    Wp = dram_in("Wp", [NCC, 2, 128, 16, 128])
    Wo = dram_in("Wo", [OC, 4, 128, 16, 128])
    A_ls = dram_in("A_ls", [2, 128, 16, 128])
    B_as = dram_in("B_as", [2, 128, 16, 128])
    WrT = dram_in("WrT", [HID, 8])
    brv = dram_in("brv", [1, 8])
    Emat = dram_in("Emat", [8, 128])
    convw = dram_in("convw", [128, 80 * KCONV])
    convb = dram_in("convb", [128, 80])
    dtbias = dram_in("dtbias", [128, 1])
    Alog = dram_in("Alog", [128, 1])
    Dvin = dram_in("Dvin", [1, 128])
    Dpair = dram_in("Dpair", [128, 64])
    normw = dram_in("normw", [128, KK])
    selmask = dram_in("selmask", [1, 16])
    jmaskv = dram_in("jmaskv", [1, 4])

    odram = nc.dram_tensor("out", [NCH, OC, 128, CS], F32, kind="ExternalOutput").ap()
    Whi_d = nc.dram_tensor("Whi_d", [NCC, 2, 128, 16, 128], BF16).ap()
    Wlo_d = nc.dram_tensor("Wlo_d", [NCC, 2, 128, 16, 128], BF16).ap()
    WOhi_d = nc.dram_tensor("WOhi_d", [OC, 4, 128, 16, 128], BF16).ap()
    WOlo_d = nc.dram_tensor("WOlo_d", [OC, 4, 128, 16, 128], BF16).ap()
    kd = dict(kind="ExternalOutput") if debug else {}
    ydram = nc.dram_tensor("ydram", [NCH, KK, 128, CS], F32, **kd).ap()
    gdram = nc.dram_tensor("gdram", [NCH, KK, 128, CS], F32, **kd).ap()
    xmdram = nc.dram_tensor("xmdram", [NCH, DC, 128, CS], F32, **kd).ap()
    cdram = nc.dram_tensor("cdram", [NCH, G, 128, CS], F32, **kd).ap()
    gcdram = nc.dram_tensor("gcdram", [NCH, 128, CS], F32, **kd).ap()
    ccin = nc.dram_tensor("ccin", [128, INTER + 64], F32, **kd).ap()
    ccout = nc.dram_tensor("ccout", [4, 128, INTER + 64], F32, **kd).ap()

    sbt = lambda n, cols, parts=128: es.enter_context(
        nc.sbuf_tensor(n, [parts, cols], F32))

    ones1 = sbt("ones1", 128, 1)
    onescol = sbt("onescol", 1)
    ident = sbt("ident", 128)
    mask = sbt("mask", CS)
    E_sb = sbt("E_sb", 128, 8)
    WrT_sb = sbt("WrT_sb", DC * 8)
    br_b = sbt("br_b", 8)
    convw_sb = sbt("convw_sb", 80 * KCONV)
    convb_sb = sbt("convb_sb", 80)
    dtb_sb = sbt("dtb_sb", 1)
    Alog_sb = sbt("Alog_sb", 1)
    negA = sbt("negA", 1)
    Drow = sbt("Drow", 128, 1)
    D_b = sbt("D_b", 128)
    Dp_sb = sbt("Dp_sb", 64)
    normw_sb = sbt("normw_sb", KK)
    brrow = sbt("brrow", 8, 1)
    smrow = sbt("smrow", 16, 1)
    jmrow = sbt("jmrow", 4, 1)
    R_sb = sbt("R_sb", 1)
    epsb = sbt("epsb", 1)
    slocal = sbt("slocal", INTER)
    Lcols = sbt("Lcols", 4)
    coefw = sbt("coefw", 2)
    selm_b = sbt("selm_b", 16)
    jm_b = sbt("jm_b", 4)
    coefcols = sbt("coefcols", 4)
    coefb = sbt("coefb", 512)
    coefrow = None

    P1 = sbt("P1", 2 * DC * W0)
    ZOFF, XOFF = 0, DC * W0
    hb = [sbt(f"hb{i}", 16 * 128) for i in range(3)]
    AR2 = sbt("AR2", DC * CS)
    xfb = AR2[:, 0:DC * CS]
    rawsb = AR2[:, 0:G * 384]
    Btm = AR2[:, 3072:3072 + G * CS]
    _o = 5120
    dtw = AR2[:, _o:_o + CS]; _o += CS
    dtw2 = AR2[:, _o:_o + CS]; _o += CS
    dt_fm = AR2[:, _o:_o + CS]; _o += CS
    cum_fm = AR2[:, _o:_o + CS]; _o += CS
    dlast = AR2[:, _o:_o + CS]; _o += CS
    ddl_fm = AR2[:, _o:_o + CS]; _o += CS
    gce_fm = AR2[:, _o:_o + CS]; _o += CS
    dt_tm = AR2[:, _o:_o + CS]; _o += CS
    ddl_tm = AR2[:, _o:_o + CS]; _o += CS
    ncm_tm = AR2[:, _o:_o + CS]; _o += CS
    hsband = [AR2[:, _o:_o + CS], AR2[:, _o + CS:_o + 2 * CS]]
    assert _o + 2 * CS == DC * CS
    sjq = [AR2[:, 0:2048], AR2[:, 2048:4096]]
    cdec = sbt("cdec", 1)
    cdecrow = sbt("cdecrow", 128, 1)
    cdec_b = sbt("cdec_b", 128)
    cumrow = sbt("cumrow", 8 * CS, 1)
    gcrow = cumrow
    coefrow = cumrow[0:1, 0:512]
    xch = [sbt(f"xch{i}", 128) for i in range(2)]
    ecb = sbt("ecb", CS)
    d1w = sbt("d1w", 128)
    L0w = sbt("L0w", CS)
    L1w = sbt("L1w", 128)
    sT0 = sbt("sT0", CS)
    sT1 = sbt("sT1", 128)
    bp0 = d1w[:, 0:128]
    bp1 = L1w[:, 0:128]
    xbcw = [sbt(f"xbcw{i}", W0) for i in range(2)]
    xbcw2 = [dtw[:, 0:CS], dtw2[:, 0:CS]]
    cvw = [sbt(f"cvw{i}", W0) for i in range(2)]
    tails = sbt("tails", 80 * HALO)
    sqw = [c[:, 0:W0] for c in cvw]
    BCfm = sbt("BCfm", 2 * G * CS)
    BOFF, COFF = 0, G * CS
    lowsb = sbt("lowsb", W0)
    d0w = lowsb[:, 0:CS]
    srow_tm = sbt("srow_tm", 8)
    srow8 = sbt("srow8", W0, 8)
    statc = sbt("statc", 4)
    stat = [sbt(f"stat{i}", W0, 1) for i in range(4)]
    ysta = [sbt(f"ysta{i}", CS) for i in range(2)]
    gsta = [y[:, 0:CS] for y in ysta]
    xm_sb = [y[:, 0:CS] for y in ysta]
    gbsb = gsta[0][:, 0:CS]
    csb2 = gsta[1][:, 0:CS]
    t1w = sbt("t1w", W0)
    sexp_sb = t1w[:, 0:W0]
    ybuf = BCfm[:, 0:8 * CS]
    gbuf = BCfm[:, 8 * CS:16 * CS]

    psr = [es.enter_context(nc.psum_tensor(f"pk{i}", [128, 512], F32))
           for i in range(8)]

    P1bf = P1[:].bitcast(BF16)
    hbbf = [h[:].bitcast(BF16) for h in hb]
    BCbf = BCfm[:].bitcast(BF16)
    XO2 = 2 * XOFF

    em = Emitter()
    for name in ("dS", "dX", "dH0", "dH1", "dH2", "dYS0", "dYS1", "dGS0",
                 "dGS1", "dXM", "dCS", "dGC", "dFL", "dL2", "dXL", "dSJ",
                 "dCC", "dO", "eT", "eV", "eS", "eG", "cc"):
        em.add_sem(name, es.enter_context(nc.semaphore(name)))

    zb = P1[:, ZOFF:ZOFF + DC * W0].rearrange("p (c t) -> p c t", c=DC)
    xb = P1[:, XOFF:XOFF + DC * W0].rearrange("p (c t) -> p c t", c=DC)
    gfull = P1[:, 0:KK * CS].rearrange("p (c t) -> p c t", c=KK)
    z3 = P1[:, XOFF:XOFF + DC * CS].rearrange("p (d t) -> p d t", d=DC)

    em.do(V_, lambda e: e.memset(ones1[:], 1.0), inc="eV")
    em.do(V_, lambda e: e.memset(onescol[:], 1.0), inc="eV")
    em.do(V_, lambda e: e.memset(R_sb[:], 0.0), inc="eV")
    em.do(V_, lambda e: e.memset(slocal[:], 0.0), inc="eV")
    em.do(V_, lambda e: e.memset(epsb[:], EPS), inc="eV")
    em.do(G_, lambda e: e.memset(ident[:], 0.0), inc="eG")
    em.wait(G_, "eG")
    em.do(G_, lambda e: e.affine_select(
        out=ident[:], in_=ident[:], compare_op=ALU.not_equal, fill=1.0,
        base=0, pattern=[[-1, 128]], channel_multiplier=1), inc="eG")
    em.do(G_, lambda e: e.memset(mask[:], 0.0), inc="eG")
    em.wait(G_, "eG")
    em.do(G_, lambda e: e.affine_select(
        out=mask[:], in_=mask[:], compare_op=ALU.is_ge, fill=NEG,
        base=0, pattern=[[1, CS]], channel_multiplier=-1), inc="eG")
    for dst, src in [
        (E_sb[:], Emat), (convw_sb[:], convw), (convb_sb[:], convb),
        (dtb_sb[:], dtbias), (normw_sb[:], normw), (Alog_sb[:], Alog),
        (Drow[:], Dvin), (brrow[:], brv), (smrow[:], selmask),
        (jmrow[:], jmaskv), (Dp_sb[:], Dpair),
    ]:
        em.do(Y_, lambda e, d=dst, s=src: e.dma_start(out=d, in_=s),
              inc=("dS", 16))
    em.do(Y_, lambda e: e.dma_start(
        out=WrT_sb[:].rearrange("p (c j) -> p c j", j=8),
        in_=WrT.rearrange("(c p) j -> p c j", p=128)), inc=("dS", 16))
    em.wait(S_, "dS")
    em.do(S_, lambda e: e.activation(negA[:], Alog_sb[:], AF.Exp), inc="eS")
    em.wait(V_, "eS")
    em.do(V_, lambda e: e.tensor_scalar(negA[:], negA[:], -1.0, None, ALU.mult),
          inc="eV")
    em.wait(T_, "dS")
    em.wait(T_, "eV")
    for dst, src, w in [(D_b, Drow, 128), (br_b, brrow, 8),
                        (selm_b, smrow, 16), (jm_b, jmrow, 4)]:
        em.do(T_, lambda e, s=src, w=w: e.matmul(
            psr[4][:, 0:w], ones1[0:1, :], s[0:1, 0:w], start=True, stop=True),
            inc="eT")
        em.wait(V_, "eT")
        em.do(V_, lambda e, d=dst, w=w: e.tensor_copy(d[:, 0:w], psr[4][:, 0:w]),
              inc="eV")
        em.wait(T_, "eV")
    em.wait(T_, "eG")
    em.wait(V_, "eG")
    em.wait(S_, "eG")

    def ln_layer(src_of_dc, Wt, dst_of_dc):
        em.wait(T_, "eV")
        em.wait(T_, "eS")
        em.wait(S_, "eT")
        em.wait(S_, "eV")
        sq_marks = {}
        for dc in range(DC):
            if dc >= 2:
                em.wait(S_, "eT", sq_marks[dc - 2])
            em.do(S_, lambda e, dc=dc: e.activation(
                sqw[dc % 2][:, 0:Wt], src_of_dc(dc), AF.Square), inc="eS")
            em.do(T_, lambda e, dc=dc: e.matmul(
                psr[4][0:1, 0:Wt], onescol[:], src_of_dc(dc),
                start=(dc == 0), stop=(dc == DC - 1)), inc="eT")
            em.wait(T_, "eS")
            sq_marks[dc] = em.do(T_, lambda e, dc=dc: e.matmul(
                psr[5][0:1, 0:Wt], onescol[:], sqw[dc % 2][:, 0:Wt],
                start=(dc == 0), stop=(dc == DC - 1)), inc="eT")
        em.wait(V_, "eT")
        em.do(V_, lambda e: e.tensor_scalar(
            stat[0][0:1, 0:Wt], psr[4][0:1, 0:Wt], 1.0 / HID, None, ALU.mult),
            inc="eV")
        em.do(V_, lambda e: e.tensor_mul(
            stat[1][0:1, 0:Wt], stat[0][0:1, 0:Wt], stat[0][0:1, 0:Wt]),
            inc="eV")
        em.do(V_, lambda e: e.scalar_tensor_tensor(
            stat[2][0:1, 0:Wt], psr[5][0:1, 0:Wt], 1.0 / HID, stat[1][0:1, 0:Wt],
            ALU.mult, ALU.subtract), inc="eV")
        em.wait(S_, "eV")
        em.do(S_, lambda e: e.activation(
            stat[3][0:1, 0:Wt], stat[2][0:1, 0:Wt], AF.Ln,
            bias=epsb[0:1, :]), inc="eS")
        em.do(S_, lambda e: e.activation(
            stat[3][0:1, 0:Wt], stat[3][0:1, 0:Wt], AF.Exp, scale=-0.5),
            inc="eS")
        em.wait(V_, "eS")
        em.do(V_, lambda e: e.scalar_tensor_tensor(
            stat[1][0:1, 0:Wt], stat[0][0:1, 0:Wt], -1.0, stat[3][0:1, 0:Wt],
            ALU.mult, ALU.mult), inc="eV")
        em.wait(T_, "eV")
        em.wait(T_, "eS")
        em.do(T_, lambda e: e.matmul(
            psr[4][:, 0:Wt], ones1[0:1, :], stat[3][0:1, 0:Wt],
            start=True, stop=True), inc="eT")
        em.do(T_, lambda e: e.matmul(
            psr[5][:, 0:Wt], ones1[0:1, :], stat[1][0:1, 0:Wt],
            start=True, stop=True), inc="eT")
        em.wait(V_, "eT")
        for dc in range(DC):
            em.do(V_, lambda e, dc=dc: e.tensor_tensor(
                dst_of_dc(dc), src_of_dc(dc), psr[4][:, 0:Wt], ALU.mult),
                inc="eV")
            em.do(V_, lambda e, dc=dc: e.tensor_tensor(
                dst_of_dc(dc), dst_of_dc(dc), psr[5][:, 0:Wt], ALU.add),
                inc="eV")

    def ffn(Wt, zsrc, res_view):
        em.wait(T_, "eV")
        em.wait(T_, "eS")
        tbs = [(i * 128, min(128, Wt - i * 128)) for i in range((Wt + 127) // 128)]
        for tb, (t0, tw) in enumerate(tbs):
            for dc in range(DC):
                em.do(T_, lambda e, tb=tb, t0=t0, tw=tw, dc=dc: e.matmul(
                    psr[6][0:tw, 8 * tb:8 * tb + 8],
                    zsrc(dc)[:, t0:t0 + tw], WrT_sb[:, 8 * dc:8 * dc + 8],
                    start=(dc == 0), stop=(dc == DC - 1)), inc="eT")
        for tb, (t0, tw) in enumerate(tbs):
            em.wait(V_, "eT")
            em.do(V_, lambda e, tb=tb, tw=tw: e.tensor_tensor(
                srow_tm[0:tw, 0:8], psr[6][0:tw, 8 * tb:8 * tb + 8],
                br_b[0:tw, :], ALU.add), inc="eV")
            em.do(V_, lambda e, tw=tw: e.reduce_max(
                statc[0:tw, 0:1], srow_tm[0:tw, 0:8], axis=AX.X, negate=True),
                inc="eV")
            em.wait(S_, "eV")
            em.do(S_, lambda e, tw=tw: e.activation(
                srow_tm[0:tw, 0:8], srow_tm[0:tw, 0:8], AF.Exp,
                bias=statc[0:tw, 0:1]), inc="eS")
            em.wait(V_, "eS")
            em.do(V_, lambda e, tw=tw: e.reduce_sum(
                statc[0:tw, 1:2], srow_tm[0:tw, 0:8], axis=AX.X), inc="eV")
            em.do(V_, lambda e, tw=tw: e.reciprocal(
                statc[0:tw, 1:2], statc[0:tw, 1:2]), inc="eV")
            em.do(V_, lambda e, tw=tw: e.tensor_scalar(
                srow_tm[0:tw, 0:8], srow_tm[0:tw, 0:8], statc[0:tw, 1:2], 8.0,
                ALU.mult, ALU.mult), inc="eV")
            em.wait(T_, "eV")
            em.do(T_, lambda e, t0=t0, tw=tw: e.transpose(
                psr[7][0:8, t0:t0 + tw], srow_tm[0:tw, 0:8], ident[0:tw, 0:tw]),
                inc="eT")
        em.wait(V_, "eT")
        em.do(V_, lambda e: e.tensor_copy(srow8[0:8, 0:Wt], psr[7][0:8, 0:Wt]),
              inc="eV")
        em.wait(Y_, "eT")
        for q in range(2):
            em.do(Y_, lambda e, q=q: e.dma_start(
                out=hb[q][:], in_=A_ls[q].rearrange("p s c -> p (s c)")),
                inc=(f"dH{q}", 16))
        em.wait(T_, "dH0")
        em.wait(T_, "dH1")
        em.wait(T_, "eV")
        for q in range(2):
            for s in range(16):
                dc = 16 * q + s
                em.do(T_, lambda e, q=q, s=s, dc=dc: e.matmul(
                    psr[0][:, 0:Wt], hb[q][:, 128 * s:128 * s + 128],
                    zsrc(dc), start=(dc == 0), stop=(dc == DC - 1)), inc="eT")
        em.do(T_, lambda e: e.matmul(
            psr[1][:, 0:Wt], E_sb[0:8, :], srow8[0:8, 0:Wt],
            start=True, stop=True), inc="eT")
        em.wait(V_, "eT")
        em.do(V_, lambda e: e.tensor_copy(sexp_sb[:, 0:Wt], psr[1][:, 0:Wt]),
              inc="eV")
        em.do(V_, lambda e: e.tensor_tensor(
            lowsb[:, 0:Wt], psr[0][:, 0:Wt], sexp_sb[:, 0:Wt], ALU.mult),
            inc="eV")
        em.wait(Y_, "eT")
        for q in range(2):
            em.do(Y_, lambda e, q=q: e.dma_start(
                out=hb[q][:], in_=B_as[q].rearrange("p s c -> p (s c)")),
                inc=(f"dH{q}", 16))
        em.wait(T_, "dH0")
        em.wait(T_, "dH1")
        em.wait(T_, "eV")
        ev_marks = {}
        for oc in range(OC):
            q, s = oc // 16, oc % 16
            bank = 2 + oc % 2
            if oc >= 2:
                em.wait(T_, "eV", ev_marks[oc - 2])
            upm = em.do(T_, lambda e, q=q, s=s, bank=bank: e.matmul(
                psr[bank][:, 0:Wt], hb[q][:, 128 * s:128 * s + 128],
                lowsb[:, 0:Wt], start=True, stop=True), inc="eT")
            em.wait(V_, "eT", upm)
            ev_marks[oc] = em.do(V_, lambda e, oc=oc, bank=bank: e.tensor_tensor(
                res_view(oc), res_view(oc), psr[bank][:, 0:Wt], ALU.add),
                inc="eV")

    def conv_and_dest(cc_conv, bank, mmk, Wt, off, tile, dest_fn):
        wtile = xbcw[cc_conv % 2]
        cw = cvw[cc_conv % 2]
        em.wait(V_, "eS")
        if tile > 0:
            em.do(V_, lambda e: e.tensor_copy(
                wtile[:, 0:HALO],
                tails[:, HALO * cc_conv:HALO * cc_conv + HALO]), inc="eV")
        em.wait(S_, "eT", mmk)
        em.wait(S_, "eV")
        emk = em.do(S_, lambda e: e.activation(
            wtile[:, HALO - off:HALO - off + Wt], psr[bank][:, 0:Wt], AF.Copy),
            inc="eS")
        em.wait(V_, "eS", emk)
        em.do(V_, lambda e: e.tensor_copy(
            tails[:, HALO * cc_conv:HALO * cc_conv + HALO],
            wtile[:, CS:CS + HALO]), inc="eV")
        em.do(V_, lambda e: e.tensor_scalar(
            cw[:, 0:CS], wtile[:, 0:CS],
            convw_sb[:, 4 * cc_conv:4 * cc_conv + 1], None, ALU.mult), inc="eV")
        for kk_ in range(1, KCONV):
            em.do(V_, lambda e, kk_=kk_: e.scalar_tensor_tensor(
                cw[:, 0:CS], wtile[:, kk_:kk_ + CS],
                convw_sb[:, 4 * cc_conv + kk_:4 * cc_conv + kk_ + 1],
                cw[:, 0:CS], ALU.mult, ALU.add), inc="eV")
        em.do(V_, lambda e: e.tensor_scalar(
            cw[:, 0:CS], cw[:, 0:CS], convb_sb[:, cc_conv:cc_conv + 1], None,
            ALU.add), inc="eV")
        em.wait(S_, "eV")
        em.do(S_, lambda e: e.activation(
            xbcw2[cc_conv % 2][:, 0:CS], cw[:, 0:CS], AF.Exp, scale=-1.0),
            inc="eS")
        em.wait(V_, "eS")
        em.do(V_, lambda e: e.tensor_scalar(
            xbcw2[cc_conv % 2][:, 0:CS], xbcw2[cc_conv % 2][:, 0:CS], 1.0,
            None, ALU.add), inc="eV")
        em.do(V_, lambda e: e.reciprocal(
            xbcw2[cc_conv % 2][:, 0:CS], xbcw2[cc_conv % 2][:, 0:CS]),
            inc="eV")
        ev = em.do(V_, lambda e: e.tensor_mul(
            dest_fn(), cw[:, 0:CS], xbcw2[cc_conv % 2][:, 0:CS]), inc="eV")
        return ("eV", ev)

    yspill = {}
    pend = {"m": None}
    marks = {"ymk0": 0, "ymk1": 0, "upd": 0, "asm1": 0, "d1": 0, "stm": 0}
    xchsets = [xch, [ddl_fm[:, 0:128], ddl_fm[:, 128:256]]]
    sTh1 = (gce_fm[:, 0:CS], dlast[:, 0:128])

    def ssd_prep(c, m, conv_mark):
        hsb = hsband[m % 2]
        g = m // 8
        xc = xchsets[m % 2]
        em.wait(T_, "eV", conv_mark)
        em.wait(T_, "eV", marks["d1"])
        for jh in range(2):
            em.do(T_, lambda e, jh=jh, hsb=hsb: e.transpose(
                psr[4 + jh][:, 256:384],
                hsb[:, 128 * jh:128 * jh + 128], ident[:]), inc="eT")
        em.wait(V_, "eT")
        xev = None
        for jh in range(2):
            xev = em.do(V_, lambda e, jh=jh, xc=xc: e.tensor_copy(
                xc[jh][:], psr[4 + jh][:, 256:384]), inc="eV")
        em.wait(T_, "eV", xev)
        b0 = em.do(T_, lambda e, hg=(2 * m) % 8: e.matmul(
            psr[4][:, 0:CS], ones1[0:1, :],
            cumrow[0:1, CS * hg:CS * hg + CS], start=True, stop=True), inc="eT")
        b1 = em.do(T_, lambda e, hg=(2 * m + 1) % 8: e.matmul(
            psr[5][:, 0:CS], ones1[0:1, :],
            cumrow[0:1, CS * hg:CS * hg + CS], start=True, stop=True), inc="eT")
        h0, h1 = 2 * m, 2 * m + 1
        em.wait(S_, "eT", b0)
        em.wait(S_, "eV", marks["asm1"])
        e0 = em.do(S_, lambda e: e.activation(
            ecb[0:64, :], psr[4][0:64, 0:CS], AF.Exp), inc="eS")
        em.wait(S_, "eT", b1)
        e1 = em.do(S_, lambda e: e.activation(
            ecb[64:128, :], psr[5][64:128, 0:CS], AF.Exp), inc="eS")
        em.wait(V_, "eS", e0)
        em.do(V_, lambda e: e.tensor_tensor(
            d0w[:], psr[4][:, 0:CS], mask[:], ALU.add), inc="eV")
        dm = em.do(V_, lambda e: e.tensor_tensor(
            d1w[:], psr[4][:, 128:CS], mask[:, 0:128], ALU.add), inc="eV")
        em.wait(S_, "eV", dm)
        em.do(S_, lambda e, h=h0: e.activation(
            L0w[:], d0w[:], AF.Exp, bias=ncm_tm[:, h:h + 1]), inc="eS")
        lm = em.do(S_, lambda e, h=h0: e.activation(
            L1w[:], d1w[:], AF.Exp, bias=ncm_tm[:, 128 + h:129 + h]), inc="eS")
        em.wait(V_, "eS", lm)
        em.wait(V_, "eT", marks["ymk0"])
        em.do(V_, lambda e, h=h0, g=g: e.scalar_tensor_tensor(
            sT0[:], L0w[:], dt_tm[:, h:h + 1],
            rawsb[:, 384 * g:384 * g + 256], ALU.mult, ALU.mult), inc="eV")
        em.do(V_, lambda e, h=h0, g=g: e.scalar_tensor_tensor(
            sT1[:], L1w[:], dt_tm[:, 128 + h:129 + h],
            rawsb[:, 384 * g + 256:384 * g + 384], ALU.mult, ALU.mult),
            inc="eV")
        em.wait(V_, "eS", e1)
        em.do(V_, lambda e: e.tensor_tensor(
            d0w[:], psr[5][:, 0:CS], mask[:], ALU.add), inc="eV")
        dm = em.do(V_, lambda e: e.tensor_tensor(
            d1w[:], psr[5][:, 128:CS], mask[:, 0:128], ALU.add), inc="eV")
        marks["d1"] = dm
        em.wait(S_, "eV", dm)
        em.do(S_, lambda e, h=h1: e.activation(
            L0w[:], d0w[:], AF.Exp, bias=ncm_tm[:, h:h + 1]), inc="eS")
        lm = em.do(S_, lambda e, h=h1: e.activation(
            L1w[:], d1w[:], AF.Exp, bias=ncm_tm[:, 128 + h:129 + h]), inc="eS")
        em.wait(V_, "eS", lm)
        em.wait(V_, "eT", marks["ymk1"])
        if m == 0:
            em.wait(V_, "dGC")
        em.do(V_, lambda e, h=h1, g=g: e.scalar_tensor_tensor(
            sTh1[0][:], L0w[:], dt_tm[:, h:h + 1],
            rawsb[:, 384 * g:384 * g + 256], ALU.mult, ALU.mult), inc="eV")
        marks["stm"] = em.do(V_, lambda e, h=h1, g=g: e.scalar_tensor_tensor(
            sTh1[1][:], L1w[:], dt_tm[:, 128 + h:129 + h],
            rawsb[:, 384 * g + 256:384 * g + 384], ALU.mult, ALU.mult),
            inc="eV")
        pend["m"] = m

    def ssd_finish(c, m):
        hsb = hsband[m % 2]
        g = m // 8
        xc = xchsets[m % 2]
        mg = c * 64 + m
        ysta_t = ysta[m % 2]
        h0, h1 = 2 * m, 2 * m + 1
        em.wait(T_, "eV", marks["stm"])
        em.wait(T_, "eV", marks["upd"])
        for hh, (a, b) in ((0, (sT0, sT1)), (1, sTh1)):
            p0 = 64 * hh
            tp = None if hh == 0 else (0, 64)
            em.do(T_, lambda e, p0=p0, tp=tp, xc=xc, a=a: e.matmul(
                psr[6][p0:p0 + 64, 0:CS], xc[0][:, p0:p0 + 64], a[:],
                start=True, stop=False, tile_position=tp), inc="eT")
            ym = em.do(T_, lambda e, p0=p0, tp=tp, xc=xc, b=b: e.matmul(
                psr[6][p0:p0 + 64, 128:CS], xc[1][:, p0:p0 + 64], b[:],
                start=False, stop=True, tile_position=tp), inc="eT")
            if hh == 0:
                marks["ymk0"] = ym
            else:
                marks["ymk1"] = ym
            em.do(T_, lambda e, h=h0 + hh, g=g, p0=p0, tp=tp: e.matmul(
                psr[7][p0:p0 + 64, 0:CS], slocal[:, 64 * h:64 * h + 64],
                BCfm[:, COFF + CS * g:COFF + CS * g + CS],
                start=True, stop=True, tile_position=tp), inc="eT")
        stm = {}
        for hh in (0, 1):
            h = h0 + hh
            em.wait(V_, "eT", stm.get(hh - 1, 0))
            em.do(V_, lambda e, h=h, g=g: e.tensor_scalar(
                bp0[:], Btm[:, CS * g:CS * g + 128], ddl_tm[:, h:h + 1], None,
                ALU.mult), inc="eV")
            bm = em.do(V_, lambda e, h=h, g=g: e.tensor_scalar(
                bp1[:], Btm[:, CS * g + 128:CS * g + CS],
                ddl_tm[:, 128 + h:129 + h], None, ALU.mult), inc="eV")
            em.wait(T_, "eV", bm)
            bank = 6 + hh
            em.do(T_, lambda e, hh=hh, bank=bank: e.matmul(
                psr[bank][:, 256:320], bp0[:], xc[0][:, 64 * hh:64 * hh + 64],
                start=True, stop=False), inc="eT")
            stm[hh] = em.do(T_, lambda e, hh=hh, bank=bank: e.matmul(
                psr[bank][:, 256:320], bp1[:], xc[1][:, 64 * hh:64 * hh + 64],
                start=False, stop=True), inc="eT")
        em.wait(V_, "eT", stm[1])
        em.wait(V_, f"dYS{m % 2}", yspill.get(mg - 2, 0))
        if m < 2:
            em.wait(V_, f"dGS{m % 2}", gspill.get((c - 1, 62 + m), 0))
        marks["asm1"] = em.do(V_, lambda e: e.tensor_tensor(
            t1w[:, 0:CS], psr[7][:, 0:CS], ecb[:], ALU.mult), inc="eV")
        em.do(V_, lambda e: e.tensor_tensor(
            t1w[:, 0:CS], t1w[:, 0:CS], psr[6][:, 0:CS], ALU.add), inc="eV")
        em.do(V_, lambda e, m=m, ysta_t=ysta_t, hsb=hsb: e.scalar_tensor_tensor(
            ysta_t[:], hsb[:], Dp_sb[:, m:m + 1], t1w[:, 0:CS],
            ALU.mult, ALU.add), inc="eV")
        em.do(V_, lambda e, h=h0: e.scalar_tensor_tensor(
            slocal[:, 64 * h:64 * h + 64], slocal[:, 64 * h:64 * h + 64],
            cdec_b[:, h:h + 1], psr[6][:, 256:320], ALU.mult, ALU.add),
            inc="eV")
        marks["upd"] = em.do(V_, lambda e, h=h1: e.scalar_tensor_tensor(
            slocal[:, 64 * h:64 * h + 64], slocal[:, 64 * h:64 * h + 64],
            cdec_b[:, h:h + 1], psr[7][:, 256:320], ALU.mult, ALU.add),
            inc="eV")
        em.wait(G_, "eV")
        yspill[mg] = em.do(G_, lambda e, c=c, m=m, t=ysta_t: e.dma_start(
            out=ydram[c, m], in_=t[:]), inc=(f"dYS{m % 2}", 16))

    def stream3(Whi, Wlo, nslabs, rhi_of_k, rlo_of_k, Wt, nout, evac_fn):
        nk = nslabs * 16
        slabs = [(i, q) for i in range(nout) for q in range(nslabs)]
        dma_at, half_marks, evac_marks = {}, {}, {}
        em.wait(Y_, "eT")
        em.wait(T_, "eV")

        def do_mms(j):
            i, q = slabs[j]
            bank = i % 4
            if q == 0 and i >= 4:
                eng_s, val = evac_marks[i - 4]
                em.wait(T_, eng_s, val)
            em.wait(T_, f"dH{j % 3}", dma_at[j])
            mmk = None
            for s in range(16):
                k = q * 16 + s
                for ps in range(3):
                    if ps == 0:
                        lhs_off, rf = 128 * s, rhi_of_k
                    elif ps == 1:
                        lhs_off, rf = 2048 + 128 * s, rhi_of_k
                    else:
                        lhs_off, rf = 128 * s, rlo_of_k
                    mmk = em.do(T_, lambda e, j=j, lhs_off=lhs_off, k=k,
                                bank=bank, rf=rf, ps=ps: e.matmul(
                                    psr[bank][:, 0:Wt],
                                    hbbf[j % 3][:, lhs_off:lhs_off + 128],
                                    rf(k), start=(k == 0 and ps == 0),
                                    stop=(k == nk - 1 and ps == 2)), inc="eT")
            half_marks[j] = mmk
            if q == nslabs - 1:
                evac_marks[i] = evac_fn(i, bank, mmk)

        for j, (i, q) in enumerate(slabs):
            if j >= 3:
                em.wait(Y_, "eT", half_marks[j - 3])
            em.do(Y_, lambda e, i=i, q=q, buf=j % 3: e.dma_start(
                out=hbbf[buf][:, 0:2048],
                in_=Whi[i, q].rearrange("p s c -> p (s c)")),
                inc=(f"dH{j % 3}", 16))
            dma_at[j] = em.do(Y_, lambda e, i=i, q=q, buf=j % 3: e.dma_start(
                out=hbbf[buf][:, 2048:4096],
                in_=Wlo[i, q].rearrange("p s c -> p (s c)")),
                inc=(f"dH{j % 3}", 16))
            if j >= 1:
                do_mms(j - 1)
        do_mms(len(slabs) - 1)

    def stream_mm(Wsrc, nslabs, rhs_of_k, Wt, nout, evac_fn):
        nk = nslabs * 16
        slabs = [(i, q) for i in range(nout) for q in range(nslabs)]
        dma_at, half_marks, evac_marks = {}, {}, {}
        em.wait(Y_, "eT")
        em.wait(T_, "eV")

        def do_mms(j):
            i, q = slabs[j]
            bank = i % 4
            if q == 0 and i >= 4:
                eng_s, val = evac_marks[i - 4]
                em.wait(T_, eng_s, val)
            em.wait(T_, f"dH{j % 3}", dma_at[j])
            mmk = None
            for s in range(16):
                k = q * 16 + s
                mmk = em.do(T_, lambda e, j=j, s=s, k=k, bank=bank: e.matmul(
                    psr[bank][:, 0:Wt], hb[j % 3][:, 128 * s:128 * s + 128],
                    rhs_of_k(k), start=(k == 0), stop=(k == nk - 1)), inc="eT")
            half_marks[j] = mmk
            if q == nslabs - 1:
                evac_marks[i] = evac_fn(i, bank, mmk)

        for j, (i, q) in enumerate(slabs):
            if j >= 3:
                em.wait(Y_, "eT", half_marks[j - 3])
            dma_at[j] = em.do(Y_, lambda e, i=i, q=q, buf=j % 3: e.dma_start(
                out=hb[buf][:], in_=Wsrc[i, q].rearrange("p s c -> p (s c)")),
                inc=(f"dH{j % 3}", 16))
            if j >= 1:
                do_mms(j - 1)
        do_mms(len(slabs) - 1)

    split_in = [(Wp[cc, q], Whi_d[cc, q], Wlo_d[cc, q])
                for cc in range(NCC) for q in range(2)]
    split_out = [(Wo[oc, q], WOhi_d[oc, q], WOlo_d[oc, q])
                 for oc in range(OC) for q in range(4)]
    sp_dma, sp_ev, sp_st = {}, {}, {}
    sp_j = [0]

    def emit_split(split_srcs):
      for srcw, dhi, dlo in split_srcs:
        j = sp_j[0]
        sp_j[0] += 1
        if j >= 3:
            em.wait(Y_, "eV", sp_ev[j - 3])
        sp_dma[j] = em.do(Y_, lambda e, srcw=srcw, buf=j % 3: e.dma_start(
            out=hb[buf][:], in_=srcw.rearrange("p s c -> p (s c)")),
            inc=(f"dH{j % 3}", 16))
        par = j % 2
        hi_st = BCbf[:, 4096 * par:4096 * par + 2048]
        lo_st = BCbf[:, 4096 * par + 2048:4096 * par + 4096]
        em.wait(V_, f"dH{j % 3}", sp_dma[j])
        em.wait(V_, f"dYS{par}", sp_st.get(j - 2, 0))
        em.do(V_, lambda e, hi_st=hi_st, buf=j % 3: e.tensor_copy(
            hi_st, hb[buf][:]), inc="eV")
        sp_ev[j] = em.do(V_, lambda e, hi_st=hi_st, lo_st=lo_st, buf=j % 3:
                         e.tensor_tensor(lo_st, hb[buf][:], hi_st,
                                         ALU.subtract), inc="eV")
        em.wait(G_, "eV", sp_ev[j])
        em.do(G_, lambda e, dhi=dhi, hi_st=hi_st: e.dma_start(
            out=dhi.rearrange("p s c -> p (s c)"), in_=hi_st),
            inc=(f"dYS{par}", 16))
        sp_st[j] = em.do(G_, lambda e, dlo=dlo, lo_st=lo_st: e.dma_start(
            out=dlo.rearrange("p s c -> p (s c)"), in_=lo_st),
            inc=(f"dYS{par}", 16))

    emit_split(split_in)
    em.wait(Y_, "dYS0")
    em.wait(Y_, "dYS1")
    em.wait(V_, "dYS0")
    em.wait(V_, "dYS1")

    zhi = P1bf[:, XO2:XO2 + DC * W0].rearrange("p (c t) -> p c t", c=DC)
    zlo = P1bf[:, XO2 + DC * W0:XO2 + 2 * DC * W0].rearrange(
        "p (c t) -> p c t", c=DC)
    ghi = P1bf[:, 0:KK * CS].rearrange("p (c t) -> p c t", c=KK)
    glo = P1bf[:, KK * CS:2 * KK * CS].rearrange("p (c t) -> p c t", c=KK)

    gspill = {}
    xmspill = {}
    gcspill = {}
    cspill = {}
    for tile in range(NCH):
        Wt = W0 if tile == 0 else CS
        off = HALO if tile == 0 else 0
        col0 = 0 if tile == 0 else HALO + CS * tile
        zsrc = lambda dc, Wt=Wt: zb[:, dc, 0:Wt]
        xsrc = lambda dc, Wt=Wt: xb[:, dc, 0:Wt]

        em.wait(Y_, "eV")
        em.wait(Y_, "eT")
        em.wait(Y_, "dXM", xmspill.get(tile - 1, 0))
        xle = em.do(Y_, lambda e, col0=col0, Wt=Wt: e.dma_start(
            out=xb[:, :, 0:Wt],
            in_=x_fm.rearrange("(c p) t -> p c t", p=128)[:, :, col0:col0 + Wt]),
            inc=("dX", 16))
        em.wait(T_, "dX", xle)
        em.wait(S_, "dX", xle)
        em.wait(V_, "dX", xle)

        if SUB >= 1:
            ln_layer(xsrc, Wt, lambda dc, Wt=Wt: zb[:, dc, 0:Wt])
        if SUB >= 2:
            ffn(Wt, zsrc, lambda oc, Wt=Wt: xb[:, oc, 0:Wt])
        if SUB >= 3:
            em.wait(G_, "eV")
            xmspill[tile] = em.do(G_, lambda e, tile=tile, off=off: e.dma_start(
                out=xmdram[tile].rearrange("c p t -> p c t"),
                in_=xb[:, :, off:off + CS]), inc=("dXM", 16))
            ln_layer(xsrc, Wt, lambda dc, Wt=Wt: zb[:, dc, 0:Wt])
            em.wait(V_, "dXM", xmspill[tile])
            for dc in range(DC):
                em.do(V_, lambda e, dc=dc, Wt=Wt: e.tensor_copy(
                    zhi[:, dc, 0:Wt], zb[:, dc, 0:Wt]), inc="eV")
                em.do(V_, lambda e, dc=dc, Wt=Wt: e.tensor_tensor(
                    zlo[:, dc, 0:Wt], zb[:, dc, 0:Wt], zhi[:, dc, 0:Wt],
                    ALU.subtract), inc="eV")
        if SUB < 4:
            continue

        cc_order = [144] + list(range(128, 144)) + list(range(64, 128)) \
            + list(range(0, 64))
        if SUB == 4:
            cc_order = [144]
        elif SUB == 5:
            cc_order = [144] + list(range(128, 144))
        elif SUB == 6:
            cc_order = [144] + list(range(128, 144)) + list(range(64, 128))
        dma_at, half_marks, evac_done = {}, {}, {}
        em.wait(T_, "eV")
        em.wait(T_, "eS")
        em.wait(Y_, "eT")
        for idx, cc in enumerate(cc_order):
            for q in range(2):
                j = idx * 2 + q
                if j >= 3:
                    em.wait(Y_, "eT", half_marks[j - 3])
                em.do(Y_, lambda e, cc=cc, q=q, buf=j % 3: e.dma_start(
                    out=hbbf[buf][:, 0:2048],
                    in_=Whi_d[cc, q].rearrange("p s c -> p (s c)")),
                    inc=(f"dH{j % 3}", 16))
                dma_at[j] = em.do(Y_, lambda e, cc=cc, q=q, buf=j % 3: e.dma_start(
                    out=hbbf[buf][:, 2048:4096],
                    in_=Wlo_d[cc, q].rearrange("p s c -> p (s c)")),
                    inc=(f"dH{j % 3}", 16))
            bank = idx % 4
            if idx >= 4 and (idx - 4) in evac_done:
                eng_s, val = evac_done[idx - 4]
                em.wait(T_, eng_s, val)
            mmk = None
            for q in range(2):
                j = idx * 2 + q
                em.wait(T_, f"dH{j % 3}", dma_at[j])
                for s in range(16):
                    k = q * 16 + s
                    for ps in range(3):
                        lhs_off = 128 * s if ps < 2 else 2048 + 128 * s
                        rv = (zhi if ps != 1 else zlo) if ps != 2 else zhi
                        if ps == 0:
                            lhs_off, rv = 128 * s, zhi
                        elif ps == 1:
                            lhs_off, rv = 2048 + 128 * s, zhi
                        else:
                            lhs_off, rv = 128 * s, zlo
                        mmk = em.do(T_, lambda e, j=j, lhs_off=lhs_off, k=k,
                                    bank=bank, Wt=Wt, rv=rv, ps=ps:
                                    e.matmul(psr[bank][:, 0:Wt],
                                             hbbf[j % 3][:, lhs_off:lhs_off + 128],
                                             rv[:, k, 0:Wt],
                                             start=(k == 0 and ps == 0),
                                             stop=(k == 31 and ps == 2)),
                                    inc="eT")
                half_marks[j] = mmk

            if cc == 144:
                em.wait(V_, "eT", mmk)
                ev = em.do(V_, lambda e, bank=bank, off=off: e.tensor_copy(
                    dtw[:], psr[bank][:, off:off + CS]), inc="eV")
                evac_done[idx] = ("eV", ev)
                em.do(V_, lambda e: e.tensor_scalar(
                    dtw[:], dtw[:], dtb_sb[:], None, ALU.add), inc="eV")
                em.do(V_, lambda e: e.scalar_tensor_tensor(
                    dtw2[:], dtw[:], -1.0, dtw[:], ALU.mult, ALU.max), inc="eV")
                em.wait(S_, "eV")
                em.do(S_, lambda e: e.activation(
                    dtw2[:], dtw2[:], AF.Exp, scale=-1.0), inc="eS")
                em.do(S_, lambda e: e.activation(
                    dtw2[:], dtw2[:], AF.Ln, bias=1.0), inc="eS")
                em.do(V_, lambda e: e.tensor_scalar(
                    dtw[:], dtw[:], 0.0, None, ALU.max), inc="eV")
                em.wait(V_, "eS")
                em.do(V_, lambda e: e.tensor_add(dt_fm[:], dtw[:], dtw2[:]),
                      inc="eV")
                em.do(V_, lambda e: e.tensor_scalar(
                    dtw[:], dt_fm[:], negA[:], None, ALU.mult), inc="eV")
                em.do(V_, lambda e: e.tensor_tensor_scan(
                    cum_fm[:], dtw[:], dtw[:], 0.0, ALU.add, ALU.bypass),
                    inc="eV")
                em.wait(S_, "eV")
                em.wait(S_, "dGC", gcspill.get(tile - 1, 0))
                em.do(S_, lambda e: e.activation(
                    dlast[:], cum_fm[:], AF.Exp, bias=cum_fm[:, CS - 1:CS],
                    scale=-1.0), inc="eS")
                em.do(S_, lambda e: e.activation(
                    cdec[:], cum_fm[:, CS - 1:CS], AF.Exp), inc="eS")
                em.do(S_, lambda e: e.activation(
                    gce_fm[:], cum_fm[:], AF.Exp, bias=R_sb[:]), inc="eS")
                em.wait(V_, "eS")
                em.do(V_, lambda e: e.tensor_mul(ddl_fm[:], dt_fm[:], dlast[:]),
                      inc="eV")
                em.do(V_, lambda e: e.tensor_tensor(
                    R_sb[:], R_sb[:], cum_fm[:, CS - 1:CS], ALU.add), inc="eV")
                em.wait(G_, "eS")
                gcspill[tile] = em.do(G_, lambda e, tile=tile: e.dma_start(
                    out=gcdram[tile], in_=gce_fm[:]), inc=("dGC", 16))
                em.wait(T_, "eV")
                for srcv, dstv, scale in ((dt_fm, dt_tm, None),
                                          (ddl_fm, ddl_tm, None),
                                          (cum_fm, ncm_tm, -1.0)):
                    for jh in range(2):
                        em.do(T_, lambda e, srcv=srcv, jh=jh: e.transpose(
                            psr[7][:, 128 * jh:128 * jh + 128],
                            srcv[:, 128 * jh:128 * jh + 128], ident[:]),
                            inc="eT")
                    if scale is None:
                        em.wait(V_, "eT")
                        em.do(V_, lambda e, dstv=dstv: e.tensor_copy(
                            dstv[:], psr[7][:, 0:CS]), inc="eV")
                        em.wait(T_, "eV")
                    else:
                        em.wait(S_, "eT")
                        em.do(S_, lambda e, dstv=dstv, scale=scale: e.activation(
                            dstv[:], psr[7][:, 0:CS], AF.Copy, scale=scale),
                            inc="eS")
                        em.wait(T_, "eS")
                em.do(T_, lambda e: e.transpose(
                    psr[7][0:1, 0:128], cdec[:], ident[:]), inc="eT")
                em.wait(V_, "eT")
                em.do(V_, lambda e: e.tensor_copy(
                    cdecrow[0:1, :], psr[7][0:1, 0:128]), inc="eV")
                em.wait(T_, "eV")
                em.do(T_, lambda e: e.matmul(
                    psr[7][:, 0:128], ones1[0:1, :], cdecrow[0:1, :],
                    start=True, stop=True), inc="eT")
                em.wait(V_, "eT")
                em.do(V_, lambda e: e.tensor_copy(cdec_b[:], psr[7][:, 0:128]),
                      inc="eV")
                em.wait(T_, "eV")
            elif 128 <= cc < 144:
                gch = cc - 128
                dst = BCfm[:, BOFF + CS * gch:BOFF + CS * gch + CS] if gch < 8 \
                    else BCfm[:, COFF + CS * (gch - 8):COFF + CS * (gch - 8) + CS]
                em.wait(S_, "dCS", cspill.get(tile - 1, 0))
                evac_done[idx] = conv_and_dest(
                    cc - 64, bank, mmk, Wt, off, tile, lambda dst=dst: dst)
                if cc == 143:
                    em.wait(G_, "eS")
                    em.wait(G_, "eV")
                    cspill[tile] = em.do(G_, lambda e, tile=tile: e.dma_start(
                        out=cdram[tile].rearrange("g p t -> p g t"),
                        in_=BCfm[:, COFF:COFF + G * CS].rearrange(
                            "p (g t) -> p g t", g=G)), inc=("dCS", 16))
                    em.wait(T_, "eS")
                    em.wait(T_, "eV")
                    for gg in range(G):
                        bsl = BCfm[:, BOFF + CS * gg:BOFF + CS * gg + CS]
                        csl = BCfm[:, COFF + CS * gg:COFF + CS * gg + CS]
                        em.do(T_, lambda e, bsl=bsl, csl=csl: e.matmul(
                            psr[5][:, 0:CS], bsl[:, 0:128], csl,
                            start=True, stop=True), inc="eT")
                        em.do(T_, lambda e, bsl=bsl, csl=csl: e.matmul(
                            psr[5][:, CS:CS + 128], bsl[:, 128:CS],
                            csl[:, 128:CS], start=True, stop=True), inc="eT")
                        em.wait(V_, "eT")
                        em.do(V_, lambda e, gg=gg: e.tensor_copy(
                            rawsb[:, 384 * gg:384 * gg + 384], psr[5][:, 0:384]),
                            inc="eV")
                        em.wait(T_, "eV")
                        for jh in range(2):
                            em.do(T_, lambda e, bsl=bsl, jh=jh: e.transpose(
                                psr[6][:, 128 * jh:128 * jh + 128],
                                bsl[:, 128 * jh:128 * jh + 128], ident[:]),
                                inc="eT")
                        em.wait(V_, "eT")
                        em.do(V_, lambda e, gg=gg: e.tensor_copy(
                            Btm[:, CS * gg:CS * gg + CS], psr[6][:, 0:CS]),
                            inc="eV")
                        em.wait(T_, "eV")
            elif 64 <= cc < 128:
                mpair = cc - 64
                if mpair % 4 == 0:
                    h8 = mpair // 4
                    em.wait(G_, "eT")
                    em.wait(G_, "eV")
                    cr = em.do(G_, lambda e, h8=h8: e.dma_start(
                        out=cumrow[0:1, :],
                        in_=cum_fm[8 * h8:8 * h8 + 8, :]), inc=("dFL", 16))
                    em.wait(T_, "dFL", cr)
                evac_done[idx] = conv_and_dest(
                    cc - 64, bank, mmk, Wt, off, tile,
                    lambda mpair=mpair: hsband[mpair % 2][:])
                if pend["m"] is not None:
                    ssd_finish(tile, pend["m"])
                    pend["m"] = None
                ssd_prep(tile, mpair, evac_done[idx][1])
            else:
                if pend["m"] is not None:
                    ssd_finish(tile, pend["m"])
                    pend["m"] = None
                gmk = gspill.get((tile, cc - 2))
                if gmk:
                    em.wait(S_, f"dGS{cc % 2}", gmk)
                if cc < 2:
                    em.wait(S_, f"dYS{cc % 2}", yspill.get(tile * 64 + 62 + cc, 0))
                em.wait(S_, "eT", mmk)
                ev = em.do(S_, lambda e, bank=bank, off=off, cc=cc: e.activation(
                    gsta[cc % 2][:], psr[bank][:, off:off + CS], AF.Silu),
                    inc="eS")
                evac_done[idx] = ("eS", ev)
                em.wait(G_, "eS", ev)
                gspill[(tile, cc)] = em.do(
                    G_, lambda e, tile=tile, cc=cc: e.dma_start(
                        out=gdram[tile, cc], in_=gsta[cc % 2][:]),
                    inc=(f"dGS{cc % 2}", 16))

    if pend["m"] is not None:
        ssd_finish(NCH - 1, pend["m"])
        pend["m"] = None

    if PHASE < 1:
        for sname in ("dS", "dX", "dH0", "dH1", "dH2", "dYS0", "dYS1", "dGS0",
                      "dGS1", "dXM", "dCS", "dGC", "dFL", "dL2", "dXL", "dSJ",
                      "dCC", "dO"):
            em.wait(Y_, sname)
        em.wait(Y_, "eV")
        em.wait(Y_, "eT")
        em.wait(Y_, "eS")
        return em, es
    em.wait(G_, "eV")
    em.do(G_, lambda e: e.dma_start(out=ccin[:, 0:INTER], in_=slocal[:]),
          inc=("dCC", 16))
    em.do(G_, lambda e: e.dma_start(out=ccin[:, INTER:INTER + 1], in_=R_sb[:],
                                    allow_slow_non_contiguous=True),
          inc=("dCC", 16))
    em.do(G_, lambda e: e.dma_start(out=ccin[:, INTER + 1:INTER + 64],
                                    in_=slocal[:, 0:63]), inc=("dCC", 16))
    em.wait(G_, "dCC")
    em.do(G_, lambda e: e.collective_compute(
        "AllGather", ALU.bypass, replica_groups=GRP_RG,
        ins=[ccin.opt()], outs=[ccout.opt()]), inc="cc")
    em.wait(V_, "eS")
    em.wait(Y_, "eT")
    em.wait(V_, "dCS")
    em.wait(V_, "eT")
    emit_split(split_out)
    em.wait(G_, "cc")
    for j in range(4):
        em.do(G_, lambda e, j=j: e.dma_start(
            out=Lcols[:, j:j + 1], in_=ccout[j][:, INTER:INTER + 1],
            allow_slow_non_contiguous=True), inc=("dL2", 16))
    em.wait(V_, "dL2")
    em.wait(V_, "eS")
    for j in range(4):
        em.do(V_, lambda e, j=j: e.tensor_scalar(
            coefw[:, 0:1], Lcols[:, 0:1], selm_b[:, 4 * j:4 * j + 1], None,
            ALU.mult), inc="eV")
        for mm_ in range(1, 4):
            em.do(V_, lambda e, j=j, mm_=mm_: e.scalar_tensor_tensor(
                coefw[:, 0:1], Lcols[:, mm_:mm_ + 1],
                selm_b[:, 4 * j + mm_:4 * j + mm_ + 1], coefw[:, 0:1],
                ALU.mult, ALU.add), inc="eV")
        em.wait(S_, "eV")
        em.do(S_, lambda e: e.activation(coefw[:, 1:2], coefw[:, 0:1], AF.Exp),
              inc="eS")
        em.wait(V_, "eS")
        em.do(V_, lambda e, j=j: e.tensor_scalar(
            coefcols[:, j:j + 1], coefw[:, 1:2], jm_b[:, j:j + 1], None,
            ALU.mult), inc="eV")
    em.wait(G_, "eV")
    em.do(G_, lambda e: e.dma_start(out=coefrow[0:1, :], in_=coefcols[:]),
          inc=("dFL", 16))
    em.wait(T_, "dFL")
    em.wait(T_, "eV")
    for j in range(4):
        em.do(T_, lambda e, j=j: e.matmul(
            psr[4][:, 0:128], ones1[0:1, :],
            coefrow[0:1, :].rearrange("o (h f) -> o h f", f=4)[:, :, j],
            start=True, stop=True), inc="eT")
        em.wait(V_, "eT")
        em.do(V_, lambda e, j=j: e.tensor_copy(
            coefb[:, 128 * j:128 * j + 128], psr[4][:, 0:128]), inc="eV")
        em.wait(T_, "eV")
    em.do(V_, lambda e: e.memset(slocal[:], 0.0), inc="eV")
    for j in range(4):
        for qq in range(4):
            em.wait(G_, "eV")
            ld = em.do(G_, lambda e, j=j, qq=qq: e.dma_start(
                out=sjq[qq % 2][:],
                in_=ccout[j][:, 2048 * qq:2048 * qq + 2048]), inc=("dSJ", 16))
            em.wait(V_, "dSJ", ld)
            for hh in range(32):
                h = 32 * qq + hh
                em.do(V_, lambda e, j=j, qq=qq, hh=hh, h=h: e.scalar_tensor_tensor(
                    slocal[:, 64 * h:64 * h + 64],
                    sjq[qq % 2][:, 64 * hh:64 * hh + 64],
                    coefb[:, 128 * j + h:128 * j + h + 1],
                    slocal[:, 64 * h:64 * h + 64], ALU.mult, ALU.add), inc="eV")

    for sname in ("dYS0", "dYS1"):
        em.wait(G_, sname)
        em.wait(Y_, sname)
        em.wait(V_, sname)
    if PHASE < 2:
        for sname in ("dS", "dX", "dH0", "dH1", "dH2", "dYS0", "dYS1", "dGS0",
                      "dGS1", "dXM", "dCS", "dGC", "dFL", "dL2", "dXL", "dSJ",
                      "dCC", "dO"):
            em.wait(Y_, sname)
        em.wait(Y_, "eV")
        em.wait(Y_, "eT")
        em.wait(Y_, "eS")
        em.wait(Y_, "cc")
        return em, es
    for sname in ("dYS0", "dYS1", "dGS0", "dGS1", "dCS", "dGC", "dXM"):
        em.wait(G_, sname)
    for c in range(NCH):
        for g in range(G):
            em.wait(G_, "eV")
            em.wait(G_, "eT")
            em.do(G_, lambda e, c=c, g=g: e.dma_start(
                out=ybuf[:].rearrange("p (k t) -> p k t", k=8),
                in_=ydram[c, 8 * g:8 * g + 8].rearrange("k p t -> p k t")),
                inc=("dL2", 16))
            em.do(G_, lambda e, c=c, g=g: e.dma_start(
                out=gbuf[:].rearrange("p (k t) -> p k t", k=8),
                in_=gdram[c, 8 * g:8 * g + 8].rearrange("k p t -> p k t")),
                inc=("dL2", 16))
            em.do(G_, lambda e, c=c, g=g: e.dma_start(
                out=csb2[:], in_=cdram[c, g]), inc=("dL2", 16))
            em.wait(T_, "dL2")
            em.wait(V_, "dL2")
            for mp in range(8):
                if mp % 4 == 0:
                    em.wait(G_, "eT")
                    ld2 = em.do(G_, lambda e, c=c, g=g, mp=mp: e.dma_start(
                        out=gcrow[0:1, :],
                        in_=gcdram[c, 16 * g + 2 * mp:16 * g + 2 * mp + 8, :]),
                        inc=("dFL", 16))
                    em.wait(T_, "dFL", ld2)
                em.wait(T_, "eV")
                for hh in (0, 1):
                    h = 16 * g + 2 * mp + hh
                    hg = (2 * mp + hh) % 8
                    p0 = 64 * hh
                    tp = None if hh == 0 else (0, 64)
                    em.do(T_, lambda e, hg=hg, p0=p0, tp=tp: e.matmul(
                        psr[4][p0:p0 + 64, 0:CS], ones1[0:1, 0:64],
                        gcrow[0:1, CS * hg:CS * hg + CS], start=True,
                        stop=True, tile_position=tp), inc="eT")
                    em.do(T_, lambda e, h=h, p0=p0, tp=tp: e.matmul(
                        psr[5][p0:p0 + 64, 0:CS], slocal[:, 64 * h:64 * h + 64],
                        csb2[:], start=True, stop=True, tile_position=tp),
                        inc="eT")
                em.wait(V_, "eT")
                em.do(V_, lambda e: e.tensor_copy(gbsb[:], psr[4][:, 0:CS]),
                      inc="eV")
                em.do(V_, lambda e: e.tensor_tensor(
                    t1w[:, 0:CS], psr[5][:, 0:CS], gbsb[:], ALU.mult), inc="eV")
                em.do(V_, lambda e, mp=mp: e.tensor_tensor(
                    t1w[:, 0:CS], t1w[:, 0:CS], ybuf[:, CS * mp:CS * mp + CS],
                    ALU.add), inc="eV")
                em.do(V_, lambda e, mp=mp: e.tensor_tensor(
                    ybuf[:, CS * mp:CS * mp + CS], t1w[:, 0:CS],
                    gbuf[:, CS * mp:CS * mp + CS], ALU.mult), inc="eV")
            sqm = {}
            em.wait(T_, "eV")
            em.wait(S_, "eV")
            em.wait(S_, "eT")
            for mp in range(8):
                if mp >= 2:
                    em.wait(S_, "eT", sqm[mp - 2])
                em.do(S_, lambda e, mp=mp: e.activation(
                    sqw[mp % 2][:, 0:CS], ybuf[:, CS * mp:CS * mp + CS],
                    AF.Square), inc="eS")
                em.wait(T_, "eS")
                sqm[mp] = em.do(T_, lambda e, mp=mp: e.matmul(
                    psr[6][0:1, 0:CS], onescol[:], sqw[mp % 2][:, 0:CS],
                    start=(mp == 0), stop=(mp == 7)), inc="eT")
            em.wait(V_, "eT")
            em.do(V_, lambda e: e.tensor_scalar(
                stat[0][0:1, 0:CS], psr[6][0:1, 0:CS], 1.0 / (16 * P), None,
                ALU.mult), inc="eV")
            em.wait(S_, "eV")
            em.do(S_, lambda e: e.activation(
                stat[0][0:1, 0:CS], stat[0][0:1, 0:CS], AF.Ln,
                bias=epsb[0:1, :]), inc="eS")
            em.do(S_, lambda e: e.activation(
                stat[0][0:1, 0:CS], stat[0][0:1, 0:CS], AF.Exp, scale=-0.5),
                inc="eS")
            em.wait(T_, "eV")
            em.wait(T_, "eS")
            em.do(T_, lambda e: e.matmul(
                psr[4][:, 0:CS], ones1[0:1, :], stat[0][0:1, 0:CS],
                start=True, stop=True), inc="eT")
            em.wait(V_, "eT")
            for mp in range(8):
                kkc = 8 * g + mp
                em.do(V_, lambda e, mp=mp, kkc=kkc: e.scalar_tensor_tensor(
                    t1w[:, 0:CS], ybuf[:, CS * mp:CS * mp + CS],
                    normw_sb[:, kkc:kkc + 1], psr[4][:, 0:CS],
                    ALU.mult, ALU.mult), inc="eV")
                em.do(V_, lambda e, kkc=kkc: e.tensor_copy(
                    ghi[:, kkc, :], t1w[:, 0:CS]), inc="eV")
                em.do(V_, lambda e, kkc=kkc: e.tensor_tensor(
                    glo[:, kkc, :], t1w[:, 0:CS], ghi[:, kkc, :],
                    ALU.subtract), inc="eV")

        def oevac(i, bank, mmark, c=c):
            par = i % 2
            em.wait(G_, "eV")
            ldm = em.do(G_, lambda e, c=c, i=i, par=par: e.dma_start(
                out=xm_sb[par][:], in_=xmdram[c, i]), inc=("dXL", 16))
            em.wait(V_, "dXL", ldm)
            em.wait(V_, "eT", mmark)
            em.wait(V_, "dO")
            return ("eV", em.do(V_, lambda e, i=i, bank=bank, par=par:
                                e.tensor_tensor(xfb[:, CS * i:CS * i + CS],
                                                psr[bank][:, 0:CS],
                                                xm_sb[par][:], ALU.add),
                                inc="eV"))

        stream3(WOhi_d, WOlo_d, 4,
                lambda k: ghi[:, k, :], lambda k: glo[:, k, :], CS, OC, oevac)

        ln_layer(lambda dc: xfb[:, CS * dc:CS * dc + CS], CS,
                 lambda dc: z3[:, dc, :])
        ffn(CS, lambda dc: z3[:, dc, :],
            lambda oc: xfb[:, CS * oc:CS * oc + CS])
        em.wait(Y_, "eV")
        for oc in range(OC):
            em.do(Y_, lambda e, c=c, oc=oc: e.dma_start(
                out=odram[c, oc], in_=xfb[:, CS * oc:CS * oc + CS]),
                inc=("dO", 16))

    for sname in ("dS", "dX", "dH0", "dH1", "dH2", "dYS0", "dYS1", "dGS0",
                  "dGS1", "dXM", "dCS", "dGC", "dFL", "dL2", "dXL", "dSJ",
                  "dCC", "dO"):
        em.wait(Y_, sname)
    return em, es


def _emit(nc, em, es):
    with es:
        blk = es.enter_context(nc.Block())

        @blk.tensor
        def _(t):
            for fn in em.ops[T_]:
                fn(t)

        @blk.vector
        def _(v):
            for fn in em.ops[V_]:
                fn(v)

        @blk.scalar
        def _(s):
            for fn in em.ops[S_]:
                fn(s)

        @blk.gpsimd
        def _(g):
            for fn in em.ops[G_]:
                fn(g)

        @blk.sync
        def _(y):
            for fn in em.ops[Y_]:
                fn(y)



def _prep_shared(i):
    out = {}
    W = np.ascontiguousarray(np.asarray(i["L0_attn_in_proj_w"], np.float32))
    out["Wp"] = np.ascontiguousarray(
        W.reshape(NCC, 128, 2, 16, 128).transpose(0, 2, 4, 3, 1))
    Wow = np.ascontiguousarray(np.asarray(i["L0_attn_out_w"], np.float32))
    out["Wo"] = np.ascontiguousarray(
        Wow.reshape(OC, 128, 4, 16, 128).transpose(0, 2, 4, 3, 1))
    A = np.asarray(i["L0_ffn_A"], np.float32)
    A2 = np.ascontiguousarray(A.reshape(128, HID).T)
    out["A_ls"] = np.ascontiguousarray(
        A2.reshape(2, 16, 128, 128).transpose(0, 2, 1, 3))
    Bm = np.asarray(i["L0_ffn_B"], np.float32)
    B2 = np.ascontiguousarray(Bm.transpose(0, 2, 1).reshape(128, HID))
    out["B_as"] = np.ascontiguousarray(
        B2.reshape(128, 2, 16, 128).transpose(1, 0, 2, 3))
    out["WrT"] = np.ascontiguousarray(np.asarray(i["L0_ffn_Wr"], np.float32).T)
    out["brv"] = np.ascontiguousarray(
        np.asarray(i["L0_ffn_br"], np.float32).reshape(1, 8))
    E = np.zeros((8, 128), np.float32)
    for j in range(8):
        E[j, 16 * j:16 * j + 16] = 1.0
    out["Emat"] = E
    out["convw"] = np.ascontiguousarray(
        np.asarray(i["L0_attn_conv_w"], np.float32)
        .reshape(80, 128, KCONV).transpose(1, 0, 2).reshape(128, 80 * KCONV))
    out["convb"] = np.ascontiguousarray(
        np.asarray(i["L0_attn_conv_b"], np.float32).reshape(80, 128).T)
    out["dtbias"] = np.ascontiguousarray(
        np.asarray(i["L0_attn_dt_bias"], np.float32).reshape(128, 1))
    out["Alog"] = np.ascontiguousarray(
        np.asarray(i["L0_attn_A_log"], np.float32).reshape(128, 1))
    Dv = np.asarray(i["L0_attn_D"], np.float32)
    out["Dvin"] = np.ascontiguousarray(Dv.reshape(1, 128))
    out["Dpair"] = np.ascontiguousarray(np.concatenate(
        [np.broadcast_to(Dv[0::2], (64, 64)), np.broadcast_to(Dv[1::2], (64, 64))],
        0))
    out["normw"] = np.ascontiguousarray(
        np.asarray(i["L0_attn_norm_w"], np.float32).reshape(KK, 128).T)
    return out


def _prep_core(x_flat, k):
    t0 = TOK * k
    if k % 4 == 0:
        halo = np.zeros((HALO, HID), np.float32)
    else:
        halo = x_flat[t0 - HALO:t0]
    xc = np.concatenate([halo, x_flat[t0:t0 + TOK]], 0)
    r = k % 4
    sel = np.zeros((1, 16), np.float32)
    jm = np.zeros((1, 4), np.float32)
    for j in range(4):
        if j < r:
            jm[0, j] = 1.0
            for m_ in range(j + 1, r):
                sel[0, 4 * j + m_] = 1.0
    return {"x_fm": np.ascontiguousarray(xc.T), "selmask": sel, "jmaskv": jm}


_NC_CACHE = {}


def _get_nc():
    if "nc" not in _NC_CACHE:
        nc = bass.Bass("TRN2", target_bir_lowering=False, num_devices=NCORES)
        em, es = build(nc)
        _emit(nc, em, es)
        _NC_CACHE["nc"] = nc
    return _NC_CACHE["nc"]


def kernel(**inputs):
    x = np.asarray(inputs["L0_attn_input_states"], np.float32)
    x_flat = np.ascontiguousarray(x.reshape(B * L, HID))
    shared = _prep_shared(inputs)
    in_maps = []
    for k in range(NCORES):
        m = dict(shared)
        m.update(_prep_core(x_flat, k))
        in_maps.append(m)
    nc = _get_nc()
    res = run_bass_kernel_spmd(nc, in_maps, core_ids=list(range(NCORES)))
    outs = []
    for k in range(NCORES):
        arr = np.asarray(res.results[k]["out"])
        outs.append(arr.transpose(0, 3, 1, 2).reshape(TOK, HID))
    return np.concatenate(outs, 0).reshape(B, L, HID)
